# version 01
# baseline (speedup 1.0000x reference)
import numpy as np
import concourse.bass as bass
import concourse.mybir as mybir
from concourse.bass_utils import run_bass_kernel_spmd
from contextlib import ExitStack

F32 = mybir.dt.float32
BF16 = mybir.dt.bfloat16
AF = mybir.ActivationFunctionType
ALU = mybir.AluOpType
AX = mybir.AxisListType

D = 1024
K32 = 32
EPS = 1e-5
FP4_MID = [0.25, 0.75, 1.25, 1.75, 2.5, 3.5, 5.0]


def bc(ap_small, ap_big):
    a, _ = bass.broadcast_tensor_aps(ap_small, ap_big)
    return a


def build_stageA(T_loc, EX):
    n_ch = (T_loc + EX) // 128
    nx = T_loc + EX + 32

    nc = bass.Bass("TRN2", target_bir_lowering=False)
    x_in = nc.dram_tensor("x_in", [nx, D], F32, kind="ExternalInput").ap()
    c_WT = nc.dram_tensor("c_WT", [D, K32], F32, kind="ExternalInput").ap()
    c_b32 = nc.dram_tensor("c_b32", [K32, 1], F32, kind="ExternalInput").ap()
    z1_out = nc.dram_tensor("z1_out", [T_loc + EX, D], F32, kind="ExternalOutput").ap()
    qxb_out = nc.dram_tensor("qxb_out", [D, T_loc + EX], BF16, kind="ExternalOutput").ap()
    flatbuf = nc.dram_tensor("flatbuf", [2, 128 * 161], F32, kind="Internal").ap()

    PE_N, ACT_N, DVE_N = 6, 7, 7
    IPE, IACT, IDVE = 0, 0, 2

    def m(ci, k):
        return IPE + ci * PE_N + k

    def a(ci, k):
        return IACT + ci * ACT_N + k

    def d(ci, k):
        return IDVE + ci * DVE_N + k

    with ExitStack() as st:
        def sb(name, shape, dt=F32):
            return st.enter_context(nc.sbuf_tensor(name, shape, dt))

        def psum(name, shape, dt=F32):
            return st.enter_context(nc.psum_tensor(name, shape, dt))

        x_a = sb("x_a", [128, 3 * D])
        x_ab = sb("x_ab", [128, 3 * D], BF16)
        x_b = sb("x_b", [128, D], BF16)
        wt_sb = sb("wt_sb", [128, 8 * K32], BF16)
        b32_sb = sb("b32_sb", [K32, 1])
        e_b = sb("e_b", [K32, 128])
        e_pad = sb("e_pad", [128, 2 * 161])
        bt_sb = sb("bt_sb", [128, 2 * 160])
        b1_sb = sb("b1_sb", [128, 128], BF16)
        b2_sb = sb("b2_sb", [K32, 128], BF16)
        rinv = sb("rinv", [128, 1])
        zpre = sb("zpre", [128, D])
        z1n = sb("z1n", [128, 2 * D])
        sqdump = sb("sqdump", [128, D])
        stat = sb("stat", [128, 16])
        au = sb("au", [128, D])
        un = sb("un", [128, D])
        lad = sb("lad", [128, D])
        lad2 = sb("lad2", [128, D])
        sg = sb("sg", [128, D])
        bm = sb("bm", [128, 64])
        rb6 = sb("rb6", [128, 64])
        sc6 = sb("sc6", [128, 64])
        qx_a = sb("qx_a", [128, D], BF16)
        qxb_sb = sb("qxb_sb", [128, D], BF16)
        ident_f = sb("ident_f", [128, 128])
        ident_b = sb("ident_b", [128, 128], BF16)

        pA = psum("pA", [128, 512])
        pB = psum("pB", [128, 512])
        pS = psum("pS", [128, 128])
        pC = psum("pC", [128, 128])
        pAt0 = psum("pAt0", [128, 512])
        pAt1 = psum("pAt1", [128, 512])
        pQ0 = psum("pQ0", [128, 512], BF16)
        pQ1 = psum("pQ1", [128, 512], BF16)

        s_x = [st.enter_context(nc.semaphore(f"s_x{i}")) for i in range(3)]
        s_fw = [st.enter_context(nc.semaphore(f"s_fw{i}")) for i in range(2)]
        s_fr = [st.enter_context(nc.semaphore(f"s_fr{i}")) for i in range(2)]
        s_pe = st.enter_context(nc.semaphore("s_pe"))
        s_act = st.enter_context(nc.semaphore("s_act"))
        s_dve = st.enter_context(nc.semaphore("s_dve"))
        s_stz = [st.enter_context(nc.semaphore(f"s_stz{i}")) for i in range(2)]
        s_stq = st.enter_context(nc.semaphore("s_stq"))
        s_c = st.enter_context(nc.semaphore("s_c"))
        block = st.enter_context(nc.Block())

        @block.sync
        def _(sync):
            for ci in range(n_ch + 3):
                if ci < n_ch:
                    if ci >= 3:
                        sync.wait_ge(s_pe, m(ci - 3, 1))
                        sync.wait_ge(s_dve, d(ci - 3, 4))
                    sync.dma_start(out=x_a[:, (ci % 3) * D:(ci % 3 + 1) * D],
                                   in_=x_in[ci * 128:(ci + 1) * 128, :]).then_inc(s_x[ci % 3], 16)
                elif ci == n_ch:
                    sync.wait_ge(s_pe, m(n_ch - 3, 1))
                    sync.wait_ge(s_dve, d(n_ch - 3, 4))
                    sync.dma_start(out=x_a[0:32, (n_ch % 3) * D:(n_ch % 3) * D + D],
                                   in_=x_in[n_ch * 128:n_ch * 128 + 32, :]).then_inc(s_x[n_ch % 3], 16)
                cj = ci - 3
                if 0 <= cj < n_ch:
                    sync.wait_ge(s_dve, d(cj, 6))
                    sync.dma_start(out=z1_out[cj * 128:(cj + 1) * 128, :],
                                   in_=z1n[:, (cj % 2) * D:(cj % 2 + 1) * D]).then_inc(s_stz[cj % 2], 16)
            sync.wait_ge(s_stz[0], 16 * ((n_ch + 1) // 2))
            sync.wait_ge(s_stz[1], 16 * (n_ch // 2))
            sync.wait_ge(s_stq, 16 * n_ch)

        @block.gpsimd
        def _(g):
            g.dma_start(out=wt_sb[:].rearrange("p (j k) -> p j k", k=K32),
                        in_=c_WT.rearrange("(j p) k -> p j k", p=128)).then_inc(s_c, 16)
            g.dma_start(out=b32_sb[:], in_=c_b32).then_inc(s_c, 16)
            g.memset(e_pad[:], 0.0)
            g.iota(sqdump[:, 0:128], [[1, 128]], base=0, channel_multiplier=-1,
                   allow_small_or_imprecise_dtypes=True)
            g.drain().then_inc(s_dve, 1)
            for ci in range(n_ch):
                g.wait_ge(s_dve, d(ci, 2))
                g.dma_start(out=flatbuf[ci % 2, :],
                            in_=e_pad[:, (ci % 2) * 161:(ci % 2) * 161 + 161]).then_inc(s_fw[ci % 2], 16)
                g.wait_ge(s_fw[ci % 2], 16 * (ci // 2 + 1))
                src = bass.AP(flatbuf.tensor, flatbuf.offset + (ci % 2) * (128 * 161),
                              [[160, 128], [1, 160]])
                if ci >= 2:
                    g.wait_ge(s_pe, m(ci - 2, 4))
                g.dma_start(out=bt_sb[:, (ci % 2) * 160:(ci % 2) * 160 + 160],
                            in_=src).then_inc(s_fr[ci % 2], 16)
                g.wait_ge(s_act, a(ci, 7))
                g.dma_start(
                    out=qxb_out[:, ci * 128:(ci + 1) * 128].rearrange("(j p) t -> p j t", p=128),
                    in_=qxb_sb[:].rearrange("p (j t) -> p j t", t=128),
                ).then_inc(s_stq, 16)
            g.wait_ge(s_stq, 16 * n_ch)

        @block.vector
        def _(v):
            v.wait_ge(s_c, 32)
            v.wait_ge(s_dve, 1)
            v.tensor_scalar(out=ident_f[:], in0=sqdump[:, 0:128], scalar1=0.0,
                            scalar2=None, op0=ALU.is_equal)
            v.drain()
            v.tensor_copy(ident_b[:], ident_f[:])
            v.drain().then_inc(s_dve, 1)
            for ci in range(n_ch):
                if ci == 0:
                    v.wait_ge(s_x[0], 16)
                    v.tensor_copy(x_ab[:, 0:D], x_a[:, 0:D])
                nxt = ci + 1
                v.wait_ge(s_x[nxt % 3], 16 * (nxt // 3 + 1))
                if nxt >= 3:
                    v.wait_ge(s_pe, m(nxt - 3, 5))
                if nxt < n_ch:
                    v.tensor_copy(x_ab[:, (nxt % 3) * D:(nxt % 3 + 1) * D],
                                  x_a[:, (nxt % 3) * D:(nxt % 3 + 1) * D])
                else:
                    v.tensor_copy(x_ab[0:32, (nxt % 3) * D:(nxt % 3) * D + D],
                                  x_a[0:32, (nxt % 3) * D:(nxt % 3) * D + D])
                v.drain().then_inc(s_dve, 1)
                v.wait_ge(s_pe, m(ci, 3))
                if ci >= 2:
                    v.wait_ge(s_fw[ci % 2], 16 * (ci // 2))
                ep = e_pad[:, (ci % 2) * 161:(ci % 2) * 161 + 161]
                v.tensor_copy(ep[:, 0:K32], pS[:, 0:K32])
                v.drain()
                v.tensor_reduce(out=stat[:, 0:1], in_=ep[:, 0:K32], axis=AX.X, op=ALU.add)
                v.drain()
                v.reciprocal(out=rinv[:], in_=stat[:, 0:1])
                v.drain().then_inc(s_dve, 1)
                v.wait_ge(s_pe, m(ci, 4))
                v.tensor_copy(b1_sb[:], pC[:])
                v.tensor_copy(b2_sb[:], pS[0:K32, :])
                v.drain().then_inc(s_dve, 1)
                v.wait_ge(s_pe, m(ci, 5))
                if ci >= 1:
                    v.wait_ge(s_act, a(ci - 1, 3))
                xa = x_a[:, (ci % 3) * D:(ci % 3 + 1) * D]
                v.scalar_tensor_tensor(out=zpre[:, 0:512], in0=pAt0[:], scalar=rinv[:],
                                       in1=xa[:, 0:512], op0=ALU.mult, op1=ALU.add,
                                       accum_out=stat[:, 0:1])
                v.scalar_tensor_tensor(out=zpre[:, 512:1024], in0=pAt1[:], scalar=rinv[:],
                                       in1=xa[:, 512:1024], op0=ALU.mult, op1=ALU.add,
                                       accum_out=stat[:, 1:2])
                v.drain()
                v.tensor_tensor(out=stat[:, 6:7], in0=stat[:, 0:1], in1=stat[:, 1:2], op=ALU.add)
                v.drain()
                v.tensor_scalar(out=stat[:, 6:7], in0=stat[:, 6:7], scalar1=1.0 / D,
                                scalar2=None, op0=ALU.mult)
                v.drain().then_inc(s_dve, 1)
                v.wait_ge(s_act, a(ci, 3))
                v.tensor_scalar(out=stat[:, 3:4], in0=stat[:, 2:3], scalar1=1.0 / D,
                                scalar2=None, op0=ALU.mult)
                v.tensor_tensor(out=stat[:, 7:8], in0=stat[:, 6:7], in1=stat[:, 6:7], op=ALU.mult)
                v.drain()
                v.tensor_tensor(out=stat[:, 3:4], in0=stat[:, 3:4], in1=stat[:, 7:8], op=ALU.subtract)
                v.drain()
                v.tensor_scalar(out=stat[:, 3:4], in0=stat[:, 3:4], scalar1=EPS,
                                scalar2=None, op0=ALU.add)
                v.drain()
                v.reciprocal(out=stat[:, 3:4], in_=stat[:, 3:4])
                v.drain().then_inc(s_dve, 1)
                v.wait_ge(s_act, a(ci, 4))
                if ci >= 2:
                    v.wait_ge(s_stz[ci % 2], 16 * (ci // 2))
                zn = z1n[:, (ci % 2) * D:(ci % 2 + 1) * D]
                v.tensor_scalar(out=zn, in0=zpre[:], scalar1=stat[:, 6:7],
                                scalar2=stat[:, 5:6], op0=ALU.subtract, op1=ALU.mult)
                v.drain().then_inc(s_dve, 1)
                v.wait_ge(s_act, a(ci, 5))
                if ci >= 1:
                    v.wait_ge(s_pe, m(ci - 1, 6))
                au3 = au[:].rearrange("p (nb w) -> p nb w", w=16)
                v.tensor_reduce(out=bm[:], in_=au3, axis=AX.X, op=ALU.max)
                v.drain()
                v.tensor_scalar(out=bm[:], in0=bm[:], scalar1=6e-12, scalar2=None, op0=ALU.max)
                v.drain()
                v.reciprocal(out=rb6[:], in_=bm[:])
                v.tensor_scalar(out=sc6[:], in0=bm[:], scalar1=1.0 / 6.0, scalar2=None, op0=ALU.mult)
                v.drain()
                v.tensor_scalar(out=rb6[:], in0=rb6[:], scalar1=6.0, scalar2=None, op0=ALU.mult)
                v.drain()
                un3 = un[:].rearrange("p (nb w) -> p nb w", w=16)
                rb63 = rb6[:].rearrange("p (nb w) -> p nb w", w=1)
                v.tensor_tensor(out=un3, in0=au3, in1=bc(rb63, au3), op=ALU.mult)
                v.drain()
                v.tensor_scalar(out=lad[:], in0=un[:], scalar1=FP4_MID[0], scalar2=None, op0=ALU.is_gt)
                v.drain()
                for thr in FP4_MID[1:4]:
                    v.scalar_tensor_tensor(out=lad[:], in0=un[:], scalar=thr, in1=lad[:],
                                           op0=ALU.is_gt, op1=ALU.add)
                    v.drain()
                v.tensor_scalar(out=lad2[:], in0=un[:], scalar1=FP4_MID[4], scalar2=None, op0=ALU.is_gt)
                v.drain()
                v.scalar_tensor_tensor(out=lad2[:], in0=un[:], scalar=FP4_MID[5], in1=lad2[:],
                                       op0=ALU.is_gt, op1=ALU.add)
                v.drain()
                v.tensor_scalar(out=un[:], in0=un[:], scalar1=FP4_MID[6], scalar2=2.0,
                                op0=ALU.is_gt, op1=ALU.mult)
                v.drain()
                v.tensor_tensor(out=lad2[:], in0=lad2[:], in1=un[:], op=ALU.add)
                v.drain()
                v.scalar_tensor_tensor(out=lad[:], in0=lad[:], scalar=0.5, in1=lad2[:],
                                       op0=ALU.mult, op1=ALU.add)
                v.drain()
                lad3 = lad[:].rearrange("p (nb w) -> p nb w", w=16)
                sc63 = sc6[:].rearrange("p (nb w) -> p nb w", w=1)
                v.tensor_tensor(out=lad3, in0=lad3, in1=bc(sc63, lad3), op=ALU.mult)
                v.drain()
                v.wait_ge(s_act, a(ci, 6))
                v.tensor_tensor(out=qx_a[:], in0=lad[:], in1=sg[:], op=ALU.mult)
                v.drain().then_inc(s_dve, 1)

        @block.scalar
        def _(sc):
            for ci in range(n_ch):
                sc.wait_ge(s_pe, m(ci, 1))
                sc.copy(out=x_b[:, 0:512], in_=pA[:])
                sc.copy(out=x_b[:, 512:1024], in_=pB[:])
                sc.drain().then_inc(s_act, 1)
                sc.wait_ge(s_pe, m(ci, 2))
                if ci >= 1:
                    sc.wait_ge(s_pe, m(ci - 1, 3))
                sc.activation(out=e_b[:], in_=pS[0:K32, :], func=AF.Exp,
                              bias=b32_sb[:], scale=1.0)
                sc.drain().then_inc(s_act, 1)
                sc.wait_ge(s_dve, d(ci, 4))
                sc.activation(out=sqdump[:], in_=zpre[:], func=AF.Square,
                              accum_out=stat[:, 2:3])
                sc.drain().then_inc(s_act, 1)
                sc.wait_ge(s_dve, d(ci, 5))
                sc.activation(out=stat[:, 4:5], in_=stat[:, 3:4], func=AF.Ln)
                sc.drain()
                sc.activation(out=stat[:, 5:6], in_=stat[:, 4:5], func=AF.Exp,
                              scale=0.5)
                sc.drain().then_inc(s_act, 1)
                sc.wait_ge(s_dve, d(ci, 6))
                if ci >= 1:
                    sc.wait_ge(s_dve, d(ci - 1, 7))
                zn = z1n[:, (ci % 2) * D:(ci % 2 + 1) * D]
                sc.activation(out=au[:], in_=zn, func=AF.Abs)
                sc.drain().then_inc(s_act, 1)
                sc.activation(out=sg[:], in_=zn, func=AF.Sign)
                sc.drain().then_inc(s_act, 1)
                sc.wait_ge(s_pe, m(ci, 6))
                if ci >= 1:
                    sc.wait_ge(s_stq, 16 * ci)
                sc.copy(out=qxb_sb[:, 0:512], in_=pQ0[:])
                sc.copy(out=qxb_sb[:, 512:1024], in_=pQ1[:])
                sc.drain().then_inc(s_act, 1)

        @block.tensor
        def _(te):
            te.wait_ge(s_c, 32)
            te.wait_ge(s_dve, 2)
            for ci in range(n_ch):
                te.wait_ge(s_x[ci % 3], 16 * (ci // 3 + 1))
                if ci >= 1:
                    te.wait_ge(s_act, a(ci - 1, 1))
                xa = x_a[:, (ci % 3) * D:(ci % 3 + 1) * D]
                for j in range(8):
                    dst = (pA if j < 4 else pB)[:, (j % 4) * 128:(j % 4) * 128 + 128]
                    last = te.transpose(dst, xa[:, j * 128:(j + 1) * 128], ident_f[:])
                last.then_inc(s_pe, 1)
                te.wait_ge(s_act, a(ci, 1))
                if ci >= 1:
                    te.wait_ge(s_dve, d(ci - 1, 3))
                    te.wait_ge(s_act, a(ci - 1, 2))
                for j in range(8):
                    last = te.matmul(pS[0:K32, :], wt_sb[:, j * K32:(j + 1) * K32],
                                     x_b[:, j * 128:(j + 1) * 128],
                                     start=(j == 0), stop=(j == 7))
                last.then_inc(s_pe, 1)
                te.wait_ge(s_act, a(ci, 2))
                te.transpose(pS[:, 0:K32], e_b[:], ident_f[0:K32, 0:K32]).then_inc(s_pe, 1)
                te.wait_ge(s_fr[ci % 2], 16 * (ci // 2 + 1))
                te.wait_ge(s_dve, d(ci, 2))
                bt = bt_sb[:, (ci % 2) * 160:(ci % 2) * 160 + 160]
                te.transpose(pC[:], bt[:, 0:128], ident_f[:])
                te.transpose(pS[0:K32, :], bt[:, 128:160], ident_f[:]).then_inc(s_pe, 1)
                te.wait_ge(s_dve, d(ci, 3))
                if ci >= 1:
                    te.wait_ge(s_dve, d(ci - 1, 4))
                xm = x_ab[:, (ci % 3) * D:(ci % 3 + 1) * D]
                xh = x_ab[0:32, ((ci + 1) % 3) * D:((ci + 1) % 3) * D + D]
                for h in range(2):
                    dst = (pAt0 if h == 0 else pAt1)[:]
                    te.matmul(dst, b1_sb[:], xm[:, h * 512:(h + 1) * 512], start=True, stop=False)
                    last = te.matmul(dst, b2_sb[:], xh[:, h * 512:(h + 1) * 512], start=False, stop=True)
                last.then_inc(s_pe, 1)
                te.wait_ge(s_dve, d(ci, 7))
                if ci >= 1:
                    te.wait_ge(s_act, a(ci - 1, 7))
                for j in range(8):
                    dst = (pQ0 if j < 4 else pQ1)[:, (j % 4) * 128:(j % 4) * 128 + 128]
                    last = te.transpose(dst, qx_a[:, j * 128:(j + 1) * 128], ident_b[:])
                last.then_inc(s_pe, 1)

    return nc



EX = 128


def shard_tokens(T, cid):
    T_loc = T // 4
    b = cid // 4
    pos = cid % 4
    t0 = pos * T_loc
    return b, t0, T_loc


def stageA_inputs(inp, T, cid):
    b, t0, T_loc = shard_tokens(T, cid)
    x = inp["L0_attn_x"][b]
    lo = t0 - EX
    hi = t0 + T_loc + 32
    xs = np.zeros((T_loc + EX + 32, D), np.float32)
    clo = max(lo, 0)
    chi = min(hi, T)
    xs[clo - lo: chi - lo] = x[clo:chi]
    WT = np.ascontiguousarray(inp["L0_attn_W"].T)
    b32 = np.ascontiguousarray(inp["L0_attn_b"].reshape(K32, 1))
    return {"x_in": xs, "c_WT": WT, "c_b32": b32}


def run_stageA(inp, T, n_cores=8, trace=False):
    T_loc = T // 4
    nc = build_stageA(T_loc, EX)
    in_maps = [stageA_inputs(inp, T, cid) for cid in range(n_cores)]
    res = run_bass_kernel_spmd(nc, in_maps, core_ids=list(range(n_cores)), trace=trace)
    import ml_dtypes
    B = inp["L0_attn_x"].shape[0]
    z1 = np.zeros((B, T, D), np.float32)
    qxb = np.zeros((B, D, T), ml_dtypes.bfloat16)
    for cid in range(n_cores):
        b, t0, T_loc = shard_tokens(T, cid)
        r = res.results[cid]
        z1[b, t0:t0 + T_loc] = r["z1_out"][EX:]
        qxb[b, :, t0:t0 + T_loc] = np.asarray(r["qxb_out"])[:, EX:]
    return z1, qxb, res


class Ev:

    def __init__(self):
        self.n = 0
        self.tags = {}

    def inc(self, tag=None):
        self.n += 1
        if tag is not None:
            assert tag not in self.tags, tag
            self.tags[tag] = self.n
        return self.n

    def at(self, tag):
        return self.tags[tag]


def emit_fp4_ladder(v, width, au_ap, sg_ap, out_ap, scr, out_wait=None):
    nb = width // 16
    un = scr["un"][:, 0:width]
    lad = scr["lad"][:, 0:width]
    lad2 = scr["lad2"][:, 0:width]
    bm = scr["bm"][:, 0:nb]
    rb6 = scr["rb6"][:, 0:nb]
    sc6 = scr["sc6"][:, 0:nb]
    au3 = au_ap.rearrange("p (nb w) -> p nb w", w=16)
    v.tensor_reduce(out=bm, in_=au3, axis=AX.X, op=ALU.max)
    v.drain()
    v.tensor_scalar(out=bm, in0=bm, scalar1=6e-12, scalar2=None, op0=ALU.max)
    v.drain()
    v.reciprocal(out=rb6, in_=bm)
    v.tensor_scalar(out=sc6, in0=bm, scalar1=1.0 / 6.0, scalar2=None, op0=ALU.mult)
    v.drain()
    v.tensor_scalar(out=rb6, in0=rb6, scalar1=6.0, scalar2=None, op0=ALU.mult)
    v.drain()
    un3 = un.rearrange("p (nb w) -> p nb w", w=16)
    rb63 = rb6.rearrange("p (nb w) -> p nb w", w=1)
    v.tensor_tensor(out=un3, in0=au3, in1=bc(rb63, au3), op=ALU.mult)
    v.drain()
    v.tensor_scalar(out=lad, in0=un, scalar1=FP4_MID[0], scalar2=None, op0=ALU.is_gt)
    v.drain()
    for thr in FP4_MID[1:4]:
        v.scalar_tensor_tensor(out=lad, in0=un, scalar=thr, in1=lad,
                               op0=ALU.is_gt, op1=ALU.add)
        v.drain()
    v.tensor_scalar(out=lad2, in0=un, scalar1=FP4_MID[4], scalar2=None, op0=ALU.is_gt)
    v.drain()
    v.scalar_tensor_tensor(out=lad2, in0=un, scalar=FP4_MID[5], in1=lad2,
                           op0=ALU.is_gt, op1=ALU.add)
    v.drain()
    v.tensor_scalar(out=un, in0=un, scalar1=FP4_MID[6], scalar2=2.0,
                    op0=ALU.is_gt, op1=ALU.mult)
    v.drain()
    v.tensor_tensor(out=lad2, in0=lad2, in1=un, op=ALU.add)
    v.drain()
    v.scalar_tensor_tensor(out=lad, in0=lad, scalar=0.5, in1=lad2,
                           op0=ALU.mult, op1=ALU.add)
    v.drain()
    lad3 = lad.rearrange("p (nb w) -> p nb w", w=16)
    sc63 = sc6.rearrange("p (nb w) -> p nb w", w=1)
    v.tensor_tensor(out=lad3, in0=lad3, in1=bc(sc63, lad3), op=ALU.mult)
    v.drain()
    if out_wait is not None:
        out_wait()
    v.tensor_tensor(out=out_ap, in0=lad, in1=sg_ap, op=ALU.mult)
    v.drain()




class _MockIns:
    def __init__(self, book):
        self._book = book

    def then_inc(self, sem, n=1):
        self._book._bump(sem, n)
        return self


class _MockEng:
    def __init__(self, book):
        self._book = book

    def __getattr__(self, name):
        book = self._book

        def f(*a, **k):
            return _MockIns(book)
        return f


class Book:

    def __init__(self):
        self.counts = {}
        self.tags = {}
        self.pass2 = False

    def _bump(self, sem, n):
        if not self.pass2:
            self.counts[id(sem)] = self.counts.get(id(sem), 0) + n

    def tag(self, sem, name):
        if not self.pass2:
            assert name not in self.tags, name
            self.tags[name] = self.counts.get(id(sem), 0)

    def at(self, name):
        if not self.pass2:
            return 0
        return self.tags[name]

    def total(self, sem):
        return self.counts.get(id(sem), 0)


def two_pass(block, engines, builders):
    B = Book()
    for name, fn in builders:
        fn(_MockEng(B), B)
    B.pass2 = True
    for name, fn in builders:
        dec = getattr(block, name)
        dec(lambda eng, fn=fn: fn(eng, B))
    return B


def build_stageB(T_loc, EX):
    DF = 2816
    NJT = DF // 128
    SL = [512, 512, 512, 512, 512, 256]
    SOFF = [0, 512, 1024, 1536, 2048, 2560]
    n_ch = (T_loc + EX) // 128

    nc = bass.Bass("TRN2", target_bir_lowering=False)
    z1_in = nc.dram_tensor("z1_in", [T_loc + EX, D], F32, kind="ExternalInput").ap()
    qxb_in = nc.dram_tensor("qxb_in", [D, T_loc + EX], BF16, kind="ExternalInput").ap()
    w_gate = nc.dram_tensor("w_gate", [DF, D], F32, kind="ExternalInput").ap()
    w_up = nc.dram_tensor("w_up", [DF, D], F32, kind="ExternalInput").ap()
    w_down = nc.dram_tensor("w_down", [D, DF], F32, kind="ExternalInput").ap()
    x2_out = nc.dram_tensor("x2_out", [T_loc + EX, D], F32, kind="ExternalOutput").ap()

    jobs = ([("g", r, 1024) for r in range(NJT)]
            + [("u", r, 1024) for r in range(NJT)]
            + [("d", r, DF) for r in range(8)])

    def pieces(wd):
        return [(i * 1024, min(1024, wd - i * 1024)) for i in range((wd + 1023) // 1024)]

    def nrounds(wd):
        return (wd // 128 + 7) // 8

    with ExitStack() as st:
        def sb(name, shape, dt=F32):
            return st.enter_context(nc.sbuf_tensor(name, shape, dt))

        def psum(name, shape, dt=F32):
            return st.enter_context(nc.psum_tensor(name, shape, dt))

        wgT = sb("wgT", [128, 8 * DF], BF16)
        wuT = sb("wuT", [128, 8 * DF], BF16)
        wdT = sb("wdT", [128, NJT * D], BF16)
        wload = sb("wload", [128, DF])
        qw = sb("qw", [128, DF], BF16)
        qxc = sb("qxc", [128, 2 * D], BF16)
        z1c = sb("z1c", [128, D])
        tt_sb = sb("tt_sb", [128, 512])
        u2_sb = sb("u2_sb", [128, 512])
        p_sb = sb("p_sb", [128, 512])
        h_sb = sb("h_sb", [128, 512])
        auq = sb("auq", [128, 1024])
        sgq = sb("sgq", [128, 1024])
        unq = sb("unq", [128, 1024])
        ladq = sb("ladq", [128, 1024])
        lad2q = sb("lad2q", [128, 1024])
        bmq = sb("bmq", [128, 64])
        rb6q = sb("rb6q", [128, 64])
        sc6q = sb("sc6q", [128, 64])
        hq_a = sb("hq_a", [128, DF], BF16)
        hq_b = sb("hq_b", [128, NJT * 128], BF16)
        zpre2 = sb("zpre2", [128, D])
        x2n = sb("x2n", [128, D])
        stat2 = sb("stat2", [128, 16])
        ident_b2 = sb("ident_b2", [128, 128], BF16)
        scr = dict(un=unq, lad=ladq, lad2=lad2q, bm=bmq, rb6=rb6q, sc6=sc6q)

        pG = psum("pG", [128, 512])
        pU = psum("pU", [128, 512])
        pD0 = psum("pD0", [128, 512])
        pD1 = psum("pD1", [128, 512])
        pT0 = psum("pT0", [128, 512], BF16)
        pT1 = psum("pT1", [128, 512], BF16)

        s_w = st.enter_context(nc.semaphore("s_w"))
        s_q = [st.enter_context(nc.semaphore(f"s_q{i}")) for i in range(2)]
        s_z = st.enter_context(nc.semaphore("s_z"))
        s_pe = st.enter_context(nc.semaphore("s_pe"))
        s_act = st.enter_context(nc.semaphore("s_act"))
        s_dve = st.enter_context(nc.semaphore("s_dve"))
        s_st = st.enter_context(nc.semaphore("s_st"))
        block = st.enter_context(nc.Block())

        def wslot(wi):
            return 0

        def wdest(kind, r, j):
            if kind == "g":
                return wgT[:, j * DF + r * 128: j * DF + r * 128 + 128]
            if kind == "u":
                return wuT[:, j * DF + r * 128: j * DF + r * 128 + 128]
            return wdT[:, j * D + r * 128: j * D + r * 128 + 128]

        def f_sync(sync, B):
            for wi, (kind, r, wd) in enumerate(jobs):
                if wi >= 1:
                    sync.wait_ge(s_dve, B.at(("wq", wi - 1)))
                srcw = {"g": w_gate, "u": w_up, "d": w_down}[kind][r * 128:(r + 1) * 128, :]
                sync.dma_start(out=wload[:, wslot(wi):wslot(wi) + wd],
                               in_=srcw).then_inc(s_w, 16)
            for ci in range(n_ch):
                if ci >= 1:
                    sync.wait_ge(s_dve, B.at(("zpre", ci - 1)))
                sync.dma_start(out=z1c[:],
                               in_=z1_in[ci * 128:(ci + 1) * 128, :]).then_inc(s_z, 16)
                cj = ci - 1
                if cj >= 0:
                    sync.wait_ge(s_dve, B.at(("x2n", cj)))
                    sync.dma_start(out=x2_out[cj * 128:(cj + 1) * 128, :],
                                   in_=x2n[:]).then_inc(s_st, 16)
            cj = n_ch - 1
            sync.wait_ge(s_dve, B.at(("x2n", cj)))
            sync.dma_start(out=x2_out[cj * 128:(cj + 1) * 128, :],
                           in_=x2n[:]).then_inc(s_st, 16)
            sync.wait_ge(s_st, 16 * n_ch)

        def f_gpsimd(g, B):
            g.iota(unq[:, 0:128], [[1, 128]], base=0, channel_multiplier=-1,
                   allow_small_or_imprecise_dtypes=True)
            g.drain().then_inc(s_dve, 1)
            B.tag(s_dve, "iota")
            for ci in range(n_ch):
                if ci >= 2:
                    g.wait_ge(s_pe, B.at(("gu", ci - 2, len(SL) - 1)))
                g.dma_start(
                    out=qxc[:, (ci % 2) * D:(ci % 2 + 1) * D].rearrange("p (j t) -> p j t", t=128),
                    in_=qxb_in[:, ci * 128:(ci + 1) * 128].rearrange("(j p) t -> p j t", p=128),
                ).then_inc(s_q[ci % 2], 16)

        def f_vector(v, B):
            v.wait_ge(s_dve, B.at("iota"))
            v.tensor_scalar(out=unq[:, 128:256], in0=unq[:, 0:128], scalar1=0.0,
                            scalar2=None, op0=ALU.is_equal)
            v.drain()
            v.tensor_copy(ident_b2[:], unq[:, 128:256])
            v.drain().then_inc(s_dve, 1)
            B.tag(s_dve, "ident")
            for wi, (kind, r, wd) in enumerate(jobs):
                v.wait_ge(s_w, 16 * (wi + 1))
                if wi >= 1:
                    v.wait_ge(s_pe, B.at(("wt", wi - 1)))
                for pi, (off, pw) in enumerate(pieces(wd)):
                    v.wait_ge(s_act, B.at(("wabs", wi, pi)))
                    emit_fp4_ladder(v, pw, auq[:, 0:pw], sgq[:, 0:pw],
                                    qw[:, off:off + pw], scr)
                    v.drain().then_inc(s_dve, 1)
                    B.tag(s_dve, ("wqp", wi, pi))
                B.tag(s_dve, ("wq", wi))
            for ci in range(n_ch):
                for si, sw in enumerate(SL):
                    v.wait_ge(s_pe, B.at(("gu", ci, si)))
                    v.wait_ge(s_act, B.at(("tanh", ci, si)))
                    v.tensor_scalar(out=u2_sb[:, 0:sw], in0=pU[:, 0:sw], scalar1=0.5,
                                    scalar2=None, op0=ALU.mult)
                    v.drain()
                    v.scalar_tensor_tensor(out=p_sb[:, 0:sw], in0=tt_sb[:, 0:sw], scalar=1.0,
                                           in1=u2_sb[:, 0:sw], op0=ALU.add, op1=ALU.mult)
                    v.drain()
                    v.tensor_tensor(out=h_sb[:, 0:sw], in0=p_sb[:, 0:sw], in1=pG[:, 0:sw],
                                    op=ALU.mult)
                    v.drain().then_inc(s_dve, 1)
                    B.tag(s_dve, ("h", ci, si))
                    v.wait_ge(s_act, B.at(("has", ci, si)))
                    if ci >= 1:
                        v.wait_ge(s_pe, B.at(("ht", ci - 1, si // 2)))
                    emit_fp4_ladder(v, sw, auq[:, 0:sw], sgq[:, 0:sw],
                                    hq_a[:, SOFF[si]:SOFF[si] + sw], scr)
                    v.drain().then_inc(s_dve, 1)
                    B.tag(s_dve, ("hq", ci, si))
                v.wait_ge(s_pe, B.at(("down", ci)))
                v.wait_ge(s_z, 16 * (ci + 1))
                if ci >= 1:
                    v.wait_ge(s_act, B.at(("sq", ci - 1)))
                z1v = z1c[:]
                v.scalar_tensor_tensor(out=zpre2[:, 0:512], in0=pD0[:], scalar=1.0,
                                       in1=z1v[:, 0:512], op0=ALU.mult, op1=ALU.add,
                                       accum_out=stat2[:, 0:1])
                v.scalar_tensor_tensor(out=zpre2[:, 512:1024], in0=pD1[:], scalar=1.0,
                                       in1=z1v[:, 512:1024], op0=ALU.mult, op1=ALU.add,
                                       accum_out=stat2[:, 1:2])
                v.drain()
                v.tensor_tensor(out=stat2[:, 6:7], in0=stat2[:, 0:1], in1=stat2[:, 1:2], op=ALU.add)
                v.drain()
                v.tensor_scalar(out=stat2[:, 6:7], in0=stat2[:, 6:7], scalar1=1.0 / D,
                                scalar2=None, op0=ALU.mult)
                v.drain().then_inc(s_dve, 1)
                B.tag(s_dve, ("zpre", ci))
                v.wait_ge(s_act, B.at(("sq", ci)))
                v.tensor_scalar(out=stat2[:, 3:4], in0=stat2[:, 2:3], scalar1=1.0 / D,
                                scalar2=None, op0=ALU.mult)
                v.tensor_tensor(out=stat2[:, 7:8], in0=stat2[:, 6:7], in1=stat2[:, 6:7], op=ALU.mult)
                v.drain()
                v.tensor_tensor(out=stat2[:, 3:4], in0=stat2[:, 3:4], in1=stat2[:, 7:8], op=ALU.subtract)
                v.drain()
                v.tensor_scalar(out=stat2[:, 3:4], in0=stat2[:, 3:4], scalar1=EPS,
                                scalar2=None, op0=ALU.add)
                v.drain()
                v.reciprocal(out=stat2[:, 3:4], in_=stat2[:, 3:4])
                v.drain().then_inc(s_dve, 1)
                B.tag(s_dve, ("var", ci))
                v.wait_ge(s_act, B.at(("rstd", ci)))
                if ci >= 1:
                    v.wait_ge(s_st, 16 * ci)
                v.tensor_scalar(out=x2n[:], in0=zpre2[:],
                                scalar1=stat2[:, 6:7], scalar2=stat2[:, 5:6],
                                op0=ALU.subtract, op1=ALU.mult)
                v.drain().then_inc(s_dve, 1)
                B.tag(s_dve, ("x2n", ci))

        def f_scalar(sc, B):
            for wi, (kind, r, wd) in enumerate(jobs):
                sc.wait_ge(s_w, 16 * (wi + 1))
                for pi, (off, pw) in enumerate(pieces(wd)):
                    if pi >= 1:
                        sc.wait_ge(s_dve, B.at(("wqp", wi, pi - 1)))
                    elif wi >= 1:
                        sc.wait_ge(s_dve, B.at(("wq", wi - 1)))
                    src_ap = wload[:, wslot(wi) + off:wslot(wi) + off + pw]
                    sc.activation(out=auq[:, 0:pw], in_=src_ap, func=AF.Abs)
                    sc.drain()
                    sc.activation(out=sgq[:, 0:pw], in_=src_ap, func=AF.Sign)
                    sc.drain().then_inc(s_act, 1)
                    B.tag(s_act, ("wabs", wi, pi))
                for rd in range(nrounds(wd)):
                    sc.wait_ge(s_pe, B.at(("wtr", wi, rd)))
                    for j in range(rd * 8, min(rd * 8 + 8, wd // 128)):
                        bank = pT0 if (j % 8) < 4 else pT1
                        slot = (j % 4) * 128
                        sc.copy(out=wdest(kind, r, j), in_=bank[:, slot:slot + 128])
                    sc.drain().then_inc(s_act, 1)
                    B.tag(s_act, ("wev", wi, rd))
            for ci in range(n_ch):
                for si, sw in enumerate(SL):
                    sc.wait_ge(s_pe, B.at(("gu", ci, si)))
                    sc.activation(out=tt_sb[:, 0:sw], in_=pG[:, 0:sw], func=AF.Tanh,
                                  scale=0.5)
                    sc.drain().then_inc(s_act, 1)
                    B.tag(s_act, ("tanh", ci, si))
                    sc.wait_ge(s_dve, B.at(("h", ci, si)))
                    if (ci, si) != (0, 0):
                        prev = (ci, si - 1) if si > 0 else (ci - 1, len(SL) - 1)
                        sc.wait_ge(s_dve, B.at(("hq",) + prev))
                    else:
                        sc.wait_ge(s_dve, B.at(("wq", len(jobs) - 1)))
                    sc.activation(out=auq[:, 0:sw], in_=h_sb[:, 0:sw], func=AF.Abs)
                    sc.drain()
                    sc.activation(out=sgq[:, 0:sw], in_=h_sb[:, 0:sw], func=AF.Sign)
                    sc.drain().then_inc(s_act, 1)
                    B.tag(s_act, ("has", ci, si))
                for rd in range(3):
                    sc.wait_ge(s_pe, B.at(("ht", ci, rd)))
                    if ci >= 1 and rd == 0:
                        sc.wait_ge(s_pe, B.at(("down", ci - 1)))
                    for j in range(rd * 8, min(rd * 8 + 8, NJT)):
                        bank = pT0 if (j % 8) < 4 else pT1
                        slot = (j % 4) * 128
                        sc.copy(out=hq_b[:, j * 128:(j + 1) * 128], in_=bank[:, slot:slot + 128])
                    sc.drain().then_inc(s_act, 1)
                    B.tag(s_act, ("hev", ci, rd))
                sc.wait_ge(s_dve, B.at(("zpre", ci)))
                sc.activation(out=lad2q[:, 0:D], in_=zpre2[:], func=AF.Square,
                              accum_out=stat2[:, 2:3])
                sc.drain().then_inc(s_act, 1)
                B.tag(s_act, ("sq", ci))
                sc.wait_ge(s_dve, B.at(("var", ci)))
                sc.activation(out=stat2[:, 4:5], in_=stat2[:, 3:4], func=AF.Ln)
                sc.drain()
                sc.activation(out=stat2[:, 5:6], in_=stat2[:, 4:5], func=AF.Exp, scale=0.5)
                sc.drain().then_inc(s_act, 1)
                B.tag(s_act, ("rstd", ci))

        def f_tensor(te, B):
            te.wait_ge(s_dve, B.at("ident"))
            prev_ev = None
            for wi, (kind, r, wd) in enumerate(jobs):
                njt = wd // 128
                te.wait_ge(s_dve, B.at(("wq", wi)))
                for rd in range(nrounds(wd)):
                    if prev_ev is not None:
                        te.wait_ge(s_act, B.at(prev_ev))
                    last = None
                    for j in range(rd * 8, min(rd * 8 + 8, njt)):
                        bank = pT0 if (j % 8) < 4 else pT1
                        slot = (j % 4) * 128
                        last = te.transpose(bank[:, slot:slot + 128],
                                            qw[:, j * 128:(j + 1) * 128], ident_b2[:])
                    last.then_inc(s_pe, 1)
                    B.tag(s_pe, ("wtr", wi, rd))
                    prev_ev = ("wev", wi, rd)
                B.tag(s_pe, ("wt", wi))
            for ci in range(n_ch):
                for si, sw in enumerate(SL):
                    te.wait_ge(s_q[ci % 2], 16 * (ci // 2 + 1))
                    if (ci, si) != (0, 0):
                        prev = (ci, si - 1) if si > 0 else (ci - 1, len(SL) - 1)
                        te.wait_ge(s_dve, B.at(("h",) + prev))
                    for j in range(8):
                        lhs = qxc[:, (ci % 2) * D + j * 128:(ci % 2) * D + (j + 1) * 128]
                        te.matmul(pG[:, 0:sw], lhs,
                                  wgT[:, j * DF + SOFF[si]:j * DF + SOFF[si] + sw],
                                  start=(j == 0), stop=(j == 7))
                    last = None
                    for j in range(8):
                        lhs = qxc[:, (ci % 2) * D + j * 128:(ci % 2) * D + (j + 1) * 128]
                        last = te.matmul(pU[:, 0:sw], lhs,
                                         wuT[:, j * DF + SOFF[si]:j * DF + SOFF[si] + sw],
                                         start=(j == 0), stop=(j == 7))
                    last.then_inc(s_pe, 1)
                    B.tag(s_pe, ("gu", ci, si))
                for rd in range(3):
                    te.wait_ge(s_dve, B.at(("hq", ci, min(2 * rd + 1, len(SL) - 1))))
                    te.wait_ge(s_act, B.at(prev_ev))
                    last = None
                    for j in range(rd * 8, min(rd * 8 + 8, NJT)):
                        bank = pT0 if (j % 8) < 4 else pT1
                        slot = (j % 4) * 128
                        last = te.transpose(bank[:, slot:slot + 128],
                                            hq_a[:, j * 128:(j + 1) * 128], ident_b2[:])
                    last.then_inc(s_pe, 1)
                    B.tag(s_pe, ("ht", ci, rd))
                    prev_ev = ("hev", ci, rd)
                te.wait_ge(s_act, B.at(("hev", ci, 2)))
                if ci >= 1:
                    te.wait_ge(s_dve, B.at(("zpre", ci - 1)))
                last = None
                for half in range(2):
                    for j in range(NJT):
                        lhs = hq_b[:, j * 128:(j + 1) * 128]
                        last = te.matmul((pD0 if half == 0 else pD1)[:], lhs,
                                         wdT[:, j * D + half * 512:j * D + half * 512 + 512],
                                         start=(j == 0), stop=(j == NJT - 1))
                last.then_inc(s_pe, 1)
                B.tag(s_pe, ("down", ci))

        two_pass(block, None, [("sync", f_sync), ("gpsimd", f_gpsimd),
                               ("vector", f_vector), ("scalar", f_scalar),
                               ("tensor", f_tensor)])
    return nc


def stageB_inputs(inp, z1_glob, qxb_glob, T, cid):
    b, t0, T_loc = shard_tokens(T, cid)
    lo = t0 - EX
    z1s = np.zeros((T_loc + EX, D), np.float32)
    clo = max(lo, 0)
    z1s[clo - lo:] = z1_glob[b, clo:t0 + T_loc]
    qs = np.zeros((D, T_loc + EX), qxb_glob.dtype)
    qs[:, clo - lo:] = qxb_glob[b, :, clo:t0 + T_loc]
    return {"z1_in": z1s, "qxb_in": qs,
            "w_gate": inp["L0_ffn_w_gate"], "w_up": inp["L0_ffn_w_up"],
            "w_down": inp["L0_ffn_w_down"]}


def chunk_list(T_loc, EX, W=512):
    out = []
    if EX:
        out.append((0, EX))
    for s in range(EX, T_loc + EX, W):
        out.append((s, min(W, T_loc + EX - s)))
    return out


def build_stageC1(T_loc, EX):
    CH = chunk_list(T_loc, EX)
    NCH = len(CH)

    nc = bass.Bass("TRN2", target_bir_lowering=False)
    x2_in = nc.dram_tensor("x2_in", [T_loc + EX, D], F32, kind="ExternalInput").ap()
    WkT = nc.dram_tensor("WkT", [8, 128, D], F32, kind="ExternalInput").ap()
    WvT = nc.dram_tensor("WvT", [8, 128, D], F32, kind="ExternalInput").ap()
    WrT = nc.dram_tensor("WrT", [8, 128, D], F32, kind="ExternalInput").ap()
    c5 = nc.dram_tensor("c5", [128, 40], F32, kind="ExternalInput").ap()
    bmask = nc.dram_tensor("bmask", [128, 1], F32, kind="ExternalInput").ap()
    k_out = nc.dram_tensor("k_out", [D, T_loc + EX], F32, kind="ExternalOutput").ap()
    v_out = nc.dram_tensor("v_out", [D, T_loc + EX], F32, kind="ExternalOutput").ap()
    sr_out = nc.dram_tensor("sr_out", [D, T_loc + EX], BF16, kind="ExternalOutput").ap()

    with ExitStack() as st:
        def sb(name, shape, dt=F32):
            return st.enter_context(nc.sbuf_tensor(name, shape, dt))

        def psum(name, shape, dt=F32):
            return st.enter_context(nc.psum_tensor(name, shape, dt))

        x2c = sb("x2c", [128, 2 * 4 * D])
        hn = sb("hn", [128, 4 * D])
        hb = sb("hb", [128, 8 * 512])
        dcol = sb("dcol", [128, 8 * 512])
        xkb = sb("xkb", [128, 8 * 512], BF16)
        xvb = sb("xvb", [128, 8 * 512], BF16)
        xrb = sb("xrb", [128, 8 * 512], BF16)
        wk_sb = sb("wk_sb", [128, 8 * D], BF16)
        wv_sb = sb("wv_sb", [128, 8 * D], BF16)
        wr_sb = sb("wr_sb", [128, 8 * D], BF16)
        cv = sb("cv", [128, 40])
        maskc = sb("maskc", [128, 1])
        mcol = sb("mcol", [128, 8])
        stat3 = sb("stat3", [128, 16])
        dump3 = sb("dump3", [128, D])
        hprev = sb("hprev", [128, 8])
        st_k = sb("st_k", [128, 8 * 512])
        st_v = sb("st_v", [128, 8 * 512])
        st_s = sb("st_s", [128, 8 * 512], BF16)
        tt3 = sb("tt3", [128, 512])
        ident3 = sb("ident3", [128, 128])

        pX0 = psum("pX0", [128, 512])
        pX1 = psum("pX1", [128, 512])
        pK = psum("pK", [128, 512])
        pV = psum("pV", [128, 512])
        pR = psum("pR", [128, 512])

        s_x = [st.enter_context(nc.semaphore(f"s_x{i}")) for i in range(2)]
        s_c = st.enter_context(nc.semaphore("s_c"))
        s_pe = st.enter_context(nc.semaphore("s_pe"))
        s_act = st.enter_context(nc.semaphore("s_act"))
        s_dve = st.enter_context(nc.semaphore("s_dve"))
        s_stk = st.enter_context(nc.semaphore("s_stk"))
        s_stv = st.enter_context(nc.semaphore("s_stv"))
        s_str = st.enter_context(nc.semaphore("s_str"))
        block = st.enter_context(nc.Block())

        def emit_store(sync, B, cj):
            cjs, cjw = CH[cj]
            sync.wait_ge(s_act, B.at(("ev", cj, 7)))
            sync.wait_ge(s_dve, B.at(("sr", cj, 7)))
            sync.dma_start(out=k_out[:, cjs:cjs + cjw].rearrange("(j p) t -> p j t", p=128),
                           in_=st_k[:, 0:8 * cjw].rearrange("p (j t) -> p j t", t=cjw)
                           ).then_inc(s_stk, 16)
            sync.dma_start(out=v_out[:, cjs:cjs + cjw].rearrange("(j p) t -> p j t", p=128),
                           in_=st_v[:, 0:8 * cjw].rearrange("p (j t) -> p j t", t=cjw)
                           ).then_inc(s_stv, 16)
            sync.dma_start(out=sr_out[:, cjs:cjs + cjw].rearrange("(j p) t -> p j t", p=128),
                           in_=st_s[:, 0:8 * cjw].rearrange("p (j t) -> p j t", t=cjw)
                           ).then_inc(s_str, 16)

        def f_sync(sync, B):
            for ci, (cs, cw) in enumerate(CH):
                if ci >= 2:
                    sync.wait_ge(s_dve, B.at(("hn", ci - 2)))
                sync.dma_start(out=x2c[:, (ci % 2) * 4 * D:(ci % 2) * 4 * D + (cw // 128) * D]
                               .rearrange("p (s c) -> p s c", c=D),
                               in_=x2_in[cs:cs + cw, :].rearrange("(s p) c -> p s c", p=128)
                               ).then_inc(s_x[ci % 2], 16)
                if ci >= 1:
                    emit_store(sync, B, ci - 1)
            emit_store(sync, B, NCH - 1)
            sync.wait_ge(s_stk, 16 * NCH)
            sync.wait_ge(s_stv, 16 * NCH)
            sync.wait_ge(s_str, 16 * NCH)

        def f_gpsimd(g, B):
            g.dma_start(out=wk_sb[:].rearrange("p (j c) -> p j c", c=D), in_=WkT.rearrange("j p c -> p j c")).then_inc(s_c, 16)
            g.dma_start(out=wv_sb[:].rearrange("p (j c) -> p j c", c=D), in_=WvT.rearrange("j p c -> p j c")).then_inc(s_c, 16)
            g.dma_start(out=wr_sb[:].rearrange("p (j c) -> p j c", c=D), in_=WrT.rearrange("j p c -> p j c")).then_inc(s_c, 16)
            g.dma_start(out=cv[:], in_=c5).then_inc(s_c, 16)
            g.dma_start(out=maskc[:], in_=bmask).then_inc(s_c, 16)
            g.iota(dump3[:, 0:128], [[1, 128]], base=0, channel_multiplier=-1,
                   allow_small_or_imprecise_dtypes=True)
            g.memset(hprev[:], 0.0)
            g.drain().then_inc(s_dve, 1)
            B.tag(s_dve, "init")

        def f_vector(v, B):
            v.wait_ge(s_dve, B.at("init"))
            v.wait_ge(s_c, 16 * 5)
            v.tensor_scalar(out=ident3[:], in0=dump3[:, 0:128], scalar1=0.0,
                            scalar2=None, op0=ALU.is_equal)
            v.drain().then_inc(s_dve, 1)
            B.tag(s_dve, "ident")
            for ci, (cs, cw) in enumerate(CH):
                ns = cw // 128
                xoff = (ci % 2) * 4 * D
                v.wait_ge(s_x[ci % 2], 16 * (ci // 2 + 1))
                for s in range(ns):
                    xa = x2c[:, xoff + s * D:xoff + (s + 1) * D]
                    v.tensor_reduce(out=mcol[:, s:s + 1], in_=xa, axis=AX.X, op=ALU.add)
                v.drain()
                v.tensor_scalar(out=mcol[:, 0:ns], in0=mcol[:, 0:ns], scalar1=1.0 / D,
                                scalar2=None, op0=ALU.mult)
                v.drain().then_inc(s_dve, 1)
                B.tag(s_dve, ("mean", ci))
                v.wait_ge(s_act, B.at(("sq", ci)))
                v.tensor_scalar(out=stat3[:, 0:ns], in0=stat3[:, 8:8 + ns], scalar1=1.0 / D,
                                scalar2=None, op0=ALU.mult)
                v.tensor_tensor(out=stat3[:, 4:4 + ns], in0=mcol[:, 0:ns], in1=mcol[:, 0:ns],
                                op=ALU.mult)
                v.drain()
                v.tensor_tensor(out=stat3[:, 0:ns], in0=stat3[:, 0:ns], in1=stat3[:, 4:4 + ns],
                                op=ALU.subtract)
                v.drain()
                v.tensor_scalar(out=stat3[:, 0:ns], in0=stat3[:, 0:ns], scalar1=EPS,
                                scalar2=None, op0=ALU.add)
                v.drain()
                v.reciprocal(out=stat3[:, 0:ns], in_=stat3[:, 0:ns])
                v.drain().then_inc(s_dve, 1)
                B.tag(s_dve, ("var", ci))
                v.wait_ge(s_act, B.at(("rstd", ci)))
                if ci >= 1:
                    v.wait_ge(s_pe, B.at(("xT", ci - 1, CH[ci - 1][1] // 128 - 1)))
                for s in range(ns):
                    xa = x2c[:, xoff + s * D:xoff + (s + 1) * D]
                    v.tensor_scalar(out=hn[:, s * D:(s + 1) * D], in0=xa,
                                    scalar1=mcol[:, s:s + 1], scalar2=stat3[:, 4 + s:5 + s],
                                    op0=ALU.subtract, op1=ALU.mult)
                v.drain().then_inc(s_dve, 1)
                B.tag(s_dve, ("hn", ci))
                v.wait_ge(s_act, B.at(("gb", ci, ns - 1)))
                if ci >= 1:
                    v.wait_ge(s_pe, B.at(("kvr", ci - 1, 7)))
                for j in range(8):
                    hj = hb[:, j * cw:(j + 1) * cw]
                    dj = dcol[:, j * cw:(j + 1) * cw]
                    v.tensor_tensor(out=dj[:, 1:cw], in0=hj[:, 1:cw], in1=hj[:, 0:cw - 1],
                                    op=ALU.subtract)
                    v.tensor_tensor(out=dj[:, 0:1], in0=hj[:, 0:1], in1=hprev[:, j:j + 1],
                                    op=ALU.subtract)
                v.drain()
                for mi, outb in [(2, xkb), (3, xvb), (4, xrb)]:
                    for j in range(8):
                        hj = hb[:, j * cw:(j + 1) * cw]
                        dj = dcol[:, j * cw:(j + 1) * cw]
                        ob = outb[:, j * cw:(j + 1) * cw]
                        tm = cv[:, mi * 8 + j:mi * 8 + j + 1]
                        v.scalar_tensor_tensor(out=ob[:, 1:cw], in0=dj[:, 1:cw], scalar=tm,
                                               in1=hj[:, 0:cw - 1], op0=ALU.mult, op1=ALU.add)
                        v.scalar_tensor_tensor(out=ob[:, 0:1], in0=dj[:, 0:1], scalar=tm,
                                               in1=hprev[:, j:j + 1], op0=ALU.mult, op1=ALU.add)
                v.drain()
                for j in range(8):
                    v.tensor_copy(hprev[:, j:j + 1], hb[:, j * cw + cw - 1:j * cw + cw])
                v.drain()
                if ci == 0 and EX:
                    v.tensor_scalar(out=hprev[:], in0=hprev[:], scalar1=maskc[:],
                                    scalar2=None, op0=ALU.mult)
                    v.drain()
                v.engine_nop().then_inc(s_dve, 1)
                B.tag(s_dve, ("mix", ci))
                for dj in range(8):
                    v.wait_ge(s_act, B.at(("ev", ci, dj)))
                    if ci >= 1 and dj == 0:
                        v.wait_ge(s_str, 16 * ci)
                    v.tensor_scalar(out=st_s[:, dj * cw:(dj + 1) * cw], in0=tt3[:, 0:cw],
                                    scalar1=0.5, scalar2=0.5, op0=ALU.mult, op1=ALU.add)
                    v.drain().then_inc(s_dve, 1)
                    B.tag(s_dve, ("sr", ci, dj))

        def f_scalar(sc, B):
            for ci, (cs, cw) in enumerate(CH):
                ns = cw // 128
                xoff = (ci % 2) * 4 * D
                sc.wait_ge(s_x[ci % 2], 16 * (ci // 2 + 1))
                sc.wait_ge(s_dve, B.at(("mean", ci)))
                for s in range(ns):
                    xa = x2c[:, xoff + s * D:xoff + (s + 1) * D]
                    sc.activation(out=dump3[:], in_=xa, func=AF.Square,
                                  accum_out=stat3[:, 8 + s:9 + s])
                    sc.drain()
                sc.drain().then_inc(s_act, 1)
                B.tag(s_act, ("sq", ci))
                sc.wait_ge(s_dve, B.at(("var", ci)))
                sc.activation(out=stat3[:, 4:4 + ns], in_=stat3[:, 0:ns], func=AF.Ln)
                sc.drain()
                sc.activation(out=stat3[:, 4:4 + ns], in_=stat3[:, 4:4 + ns], func=AF.Exp,
                              scale=0.5)
                sc.drain().then_inc(s_act, 1)
                B.tag(s_act, ("rstd", ci))
                for s in range(ns):
                    sc.wait_ge(s_pe, B.at(("xT", ci, s)))
                    for j in range(8):
                        bank = pX0 if j < 4 else pX1
                        slot = (j % 4) * 128
                        sc.activation(out=hb[:, j * cw + s * 128:j * cw + s * 128 + 128],
                                      in_=bank[:, slot:slot + 128], func=AF.Identity,
                                      scale=cv[:, j:j + 1], bias=cv[:, 8 + j:8 + j + 1])
                    sc.drain().then_inc(s_act, 1)
                    B.tag(s_act, ("gb", ci, s))
                for dj in range(8):
                    sc.wait_ge(s_pe, B.at(("kvr", ci, dj)))
                    prevd = (ci, dj - 1) if dj > 0 else ((ci - 1, 7) if ci > 0 else None)
                    if prevd is not None:
                        sc.wait_ge(s_dve, B.at(("sr",) + prevd))
                    if ci >= 1 and dj == 0:
                        sc.wait_ge(s_stk, 16 * ci)
                        sc.wait_ge(s_stv, 16 * ci)
                    sc.copy(out=st_k[:, dj * cw:(dj + 1) * cw], in_=pK[:, 0:cw])
                    sc.copy(out=st_v[:, dj * cw:(dj + 1) * cw], in_=pV[:, 0:cw])
                    sc.drain()
                    sc.activation(out=tt3[:, 0:cw], in_=pR[:, 0:cw], func=AF.Tanh, scale=0.5)
                    sc.drain().then_inc(s_act, 1)
                    B.tag(s_act, ("ev", ci, dj))

        def f_tensor(te, B):
            te.wait_ge(s_c, 16 * 5)
            te.wait_ge(s_dve, B.at("ident"))
            for ci, (cs, cw) in enumerate(CH):
                ns = cw // 128
                te.wait_ge(s_dve, B.at(("hn", ci)))
                for s in range(ns):
                    if ci > 0 or s > 0:
                        prev = (ci, s - 1) if s > 0 else (ci - 1, CH[ci - 1][1] // 128 - 1)
                        te.wait_ge(s_act, B.at(("gb",) + prev))
                    last = None
                    for j in range(8):
                        bank = pX0 if j < 4 else pX1
                        slot = (j % 4) * 128
                        last = te.transpose(bank[:, slot:slot + 128],
                                            hn[:, s * D + j * 128:s * D + (j + 1) * 128],
                                            ident3[:])
                    last.then_inc(s_pe, 1)
                    B.tag(s_pe, ("xT", ci, s))
                te.wait_ge(s_dve, B.at(("mix", ci)))
                for dj in range(8):
                    if True:
                        prev = (ci, dj - 1) if dj > 0 else ((ci - 1, 7) if ci > 0 else None)
                        if prev is not None:
                            te.wait_ge(s_act, B.at(("ev",) + prev))
                            te.wait_ge(s_dve, B.at(("sr",) + prev))
                    for j in range(8):
                        te.matmul(pK[:, 0:cw], wk_sb[:, j * D + dj * 128:j * D + dj * 128 + 128],
                                  xkb[:, j * cw:j * cw + cw], start=(j == 0), stop=(j == 7))
                    for j in range(8):
                        te.matmul(pV[:, 0:cw], wv_sb[:, j * D + dj * 128:j * D + dj * 128 + 128],
                                  xvb[:, j * cw:j * cw + cw], start=(j == 0), stop=(j == 7))
                    last = None
                    for j in range(8):
                        last = te.matmul(pR[:, 0:cw], wr_sb[:, j * D + dj * 128:j * D + dj * 128 + 128],
                                         xrb[:, j * cw:j * cw + cw], start=(j == 0), stop=(j == 7))
                    last.then_inc(s_pe, 1)
                    B.tag(s_pe, ("kvr", ci, dj))

        two_pass(block, None, [("sync", f_sync), ("gpsimd", f_gpsimd),
                               ("vector", f_vector), ("scalar", f_scalar),
                               ("tensor", f_tensor)])
    return nc


def stageC1_inputs(inp, x2_glob, T, cid):
    b, t0, T_loc = shard_tokens(T, cid)
    lo = t0 - EX
    clo = max(lo, 0)
    x2s = np.zeros((T_loc + EX, D), np.float32)
    x2s[clo - lo:] = x2_glob[b, clo:t0 + T_loc]

    def tileT(w):
        return np.ascontiguousarray(w.T.reshape(8, 128, D))

    def col8(vec):
        return np.ascontiguousarray(vec.reshape(8, 128).T)

    c5v = np.concatenate([col8(inp["L1_attn_ln1_g"]), col8(inp["L1_attn_ln1_b"]),
                          col8(inp["L1_attn_tm_k"]), col8(inp["L1_attn_tm_v"]),
                          col8(inp["L1_attn_tm_r"])], axis=1)
    msk = np.full((128, 1), 0.0 if cid % 4 == 0 else 1.0, np.float32)
    return {"x2_in": x2s, "WkT": tileT(inp["L1_attn_Wk"]), "WvT": tileT(inp["L1_attn_Wv"]),
            "WrT": tileT(inp["L1_attn_Wr"]), "c5": np.ascontiguousarray(c5v), "bmask": msk}


def build_stageC2(T, PIECE=4096):
    NP = T // PIECE

    nc = bass.Bass("TRN2", target_bir_lowering=False)
    k_in = nc.dram_tensor("k_in", [2, 128, T], F32, kind="ExternalInput").ap()
    v_in = nc.dram_tensor("v_in", [2, 128, T], F32, kind="ExternalInput").ap()
    td = nc.dram_tensor("td", [128, 1], F32, kind="ExternalInput").ap()
    tf = nc.dram_tensor("tf", [128, 1], F32, kind="ExternalInput").ap()
    y_out = nc.dram_tensor("y_out", [2, 128, T], BF16, kind="ExternalOutput").ap()

    with ExitStack() as st:
        def sb(name, shape, dt=F32):
            return st.enter_context(nc.sbuf_tensor(name, shape, dt))

        kp = sb("kp", [128, 2 * PIECE])
        vp = sb("vp", [128, 2 * PIECE])
        ek = sb("ek", [128, PIECE])
        ekv = sb("ekv", [128, PIECE])
        euk = sb("euk", [128, PIECE])
        As = sb("As", [128, PIECE])
        Bs = sb("Bs", [128, PIECE])
        num = sb("num", [128, PIECE])
        den = sb("den", [128, PIECE])
        yb = sb("yb", [128, PIECE], BF16)
        dmat = sb("dmat", [128, PIECE])
        cvec = sb("cvec", [128, 8])
        s_k = [st.enter_context(nc.semaphore(f"s_k{i}")) for i in range(2)]
        s_v = [st.enter_context(nc.semaphore(f"s_v{i}")) for i in range(2)]
        s_dve = st.enter_context(nc.semaphore("s_dve"))
        s_act = st.enter_context(nc.semaphore("s_act"))
        s_st = st.enter_context(nc.semaphore("s_st"))
        s_c = st.enter_context(nc.semaphore("s_c"))
        block = st.enter_context(nc.Block())

        def f_sync(sync, B):
            sync.dma_start(out=cvec[:, 0:1], in_=td).then_inc(s_c, 16)
            sync.dma_start(out=cvec[:, 1:2], in_=tf).then_inc(s_c, 16)
            for pi in range(2 * NP):
                b, pp = divmod(pi, NP)
                if pi >= 2:
                    sync.wait_ge(s_dve, B.at(("used", pi - 2)))
                sync.dma_start(out=kp[:, (pi % 2) * PIECE:(pi % 2 + 1) * PIECE],
                               in_=k_in[b, :, pp * PIECE:(pp + 1) * PIECE]).then_inc(s_k[pi % 2], 16)
                sync.dma_start(out=vp[:, (pi % 2) * PIECE:(pi % 2 + 1) * PIECE],
                               in_=v_in[b, :, pp * PIECE:(pp + 1) * PIECE]).then_inc(s_v[pi % 2], 16)
                cj = pi - 1
                if cj >= 0:
                    bj, ppj = divmod(cj, NP)
                    sync.wait_ge(s_dve, B.at(("y", cj)))
                    sync.dma_start(out=y_out[bj, :, ppj * PIECE:(ppj + 1) * PIECE],
                                   in_=yb[:]).then_inc(s_st, 16)
            cj = 2 * NP - 1
            bj, ppj = divmod(cj, NP)
            sync.wait_ge(s_dve, B.at(("y", cj)))
            sync.dma_start(out=y_out[bj, :, ppj * PIECE:(ppj + 1) * PIECE],
                           in_=yb[:]).then_inc(s_st, 16)
            sync.wait_ge(s_st, 16 * 2 * NP)

        def f_scalar(sc, B):
            sc.wait_ge(s_c, 32)
            sc.activation(out=cvec[:, 2:3], in_=cvec[:, 0:1], func=AF.Exp)
            sc.drain()
            sc.activation(out=cvec[:, 2:3], in_=cvec[:, 2:3], func=AF.Exp, scale=-1.0)
            sc.drain()
            sc.activation(out=cvec[:, 3:4], in_=cvec[:, 1:2], func=AF.Exp)
            sc.drain().then_inc(s_act, 1)
            B.tag(s_act, "consts")
            for pi in range(2 * NP):
                sc.wait_ge(s_k[pi % 2], 16 * (pi // 2 + 1))
                if pi >= 1:
                    sc.wait_ge(s_dve, B.at(("ekdone", pi - 1)))
                sc.activation(out=ek[:], in_=kp[:, (pi % 2) * PIECE:(pi % 2 + 1) * PIECE],
                              func=AF.Exp)
                sc.drain().then_inc(s_act, 1)
                B.tag(s_act, ("ek", pi))

        def f_gpsimd(g, B):
            g.memset(dmat[:], 1.0)
            g.memset(cvec[:, 4:7], 0.0)
            g.drain().then_inc(s_dve, 1)
            B.tag(s_dve, "dinit")

        def f_vector(v, B):
            v.wait_ge(s_dve, B.at("dinit"))
            v.wait_ge(s_act, B.at("consts"))
            v.tensor_scalar(out=dmat[:], in0=dmat[:], scalar1=cvec[:, 2:3],
                            scalar2=None, op0=ALU.mult)
            v.drain().then_inc(s_dve, 1)
            B.tag(s_dve, "dmat")
            for pi in range(2 * NP):
                b, pp = divmod(pi, NP)
                slot = (pi % 2) * PIECE
                v.wait_ge(s_act, B.at(("ek", pi)))
                v.wait_ge(s_v[pi % 2], 16 * (pi // 2 + 1))
                v.tensor_scalar(out=euk[:], in0=ek[:], scalar1=cvec[:, 3:4],
                                scalar2=None, op0=ALU.mult)
                v.tensor_tensor(out=ekv[:], in0=ek[:], in1=vp[:, slot:slot + PIECE],
                                op=ALU.mult)
                v.drain()
                if pp == 0:
                    v.tensor_scalar(out=cvec[:, 4:6], in0=cvec[:, 4:6], scalar1=0.0,
                                    scalar2=None, op0=ALU.mult)
                    v.drain()
                v.tensor_tensor_scan(out=As[:], data0=dmat[:], data1=ekv[:],
                                     initial=cvec[:, 4:5], op0=ALU.mult, op1=ALU.add)
                v.drain()
                v.tensor_tensor_scan(out=Bs[:], data0=dmat[:], data1=ek[:],
                                     initial=cvec[:, 5:6], op0=ALU.mult, op1=ALU.add)
                v.drain().then_inc(s_dve, 1)
                B.tag(s_dve, ("ekdone", pi))
                v.tensor_tensor(out=num[:], in0=euk[:], in1=vp[:, slot:slot + PIECE],
                                op=ALU.mult)
                v.drain()
                v.tensor_tensor(out=num[:, 1:PIECE], in0=num[:, 1:PIECE],
                                in1=As[:, 0:PIECE - 1], op=ALU.add)
                v.tensor_tensor(out=num[:, 0:1], in0=num[:, 0:1], in1=cvec[:, 4:5], op=ALU.add)
                v.drain()
                v.tensor_tensor(out=den[:, 1:PIECE], in0=euk[:, 1:PIECE],
                                in1=Bs[:, 0:PIECE - 1], op=ALU.add)
                v.tensor_tensor(out=den[:, 0:1], in0=euk[:, 0:1], in1=cvec[:, 5:6], op=ALU.add)
                v.drain()
                v.tensor_copy(cvec[:, 4:5], As[:, PIECE - 1:PIECE])
                v.tensor_copy(cvec[:, 5:6], Bs[:, PIECE - 1:PIECE])
                v.drain()
                v.reciprocal(out=den[:], in_=den[:])
                v.drain()
                if pi >= 1:
                    v.wait_ge(s_st, 16 * pi)
                v.tensor_tensor(out=yb[:], in0=num[:], in1=den[:], op=ALU.mult)
                v.drain().then_inc(s_dve, 1)
                B.tag(s_dve, ("y", pi))
                v.engine_nop().then_inc(s_dve, 1)
                B.tag(s_dve, ("used", pi))

        two_pass(block, None, [("sync", f_sync), ("gpsimd", f_gpsimd),
                               ("scalar", f_scalar), ("vector", f_vector)])
    return nc


def stageC2_inputs(k_glob, v_glob, inp, cid):
    ch0 = cid * 128
    return {"k_in": np.ascontiguousarray(k_glob[:, ch0:ch0 + 128, :]),
            "v_in": np.ascontiguousarray(v_glob[:, ch0:ch0 + 128, :]),
            "td": np.ascontiguousarray(inp["L1_attn_time_decay"][ch0:ch0 + 128].reshape(128, 1)),
            "tf": np.ascontiguousarray(inp["L1_attn_time_first"][ch0:ch0 + 128].reshape(128, 1))}


def build_stageC3a(T_loc, EX):
    CH = chunk_list(T_loc, EX)
    NCH = len(CH)

    nc = bass.Bass("TRN2", target_bir_lowering=False)
    x2_in = nc.dram_tensor("x2_in", [T_loc + EX, D], F32, kind="ExternalInput").ap()
    y_in = nc.dram_tensor("y_in", [D, T_loc + EX], BF16, kind="ExternalInput").ap()
    sr_in = nc.dram_tensor("sr_in", [D, T_loc + EX], BF16, kind="ExternalInput").ap()
    WoT = nc.dram_tensor("WoT", [8, 128, D], F32, kind="ExternalInput").ap()
    FrT = nc.dram_tensor("FrT", [8, 128, D], F32, kind="ExternalInput").ap()
    c4 = nc.dram_tensor("c4", [128, 32], F32, kind="ExternalInput").ap()
    bmask = nc.dram_tensor("bmask", [128, 1], F32, kind="ExternalInput").ap()
    s2m_out = nc.dram_tensor("s2m_out", [D, T_loc + EX], F32, kind="ExternalOutput").ap()
    gk_out = nc.dram_tensor("gk_out", [D, T_loc + EX], BF16, kind="ExternalOutput").ap()
    sig_out = nc.dram_tensor("sig_out", [D, T_loc + EX], BF16, kind="ExternalOutput").ap()

    with ExitStack() as st:
        def sb(name, shape, dt=F32):
            return st.enter_context(nc.sbuf_tensor(name, shape, dt))

        def psum(name, shape, dt=F32):
            return st.enter_context(nc.psum_tensor(name, shape, dt))

        x2c = sb("x2c", [128, 2 * 4 * D])
        yc = sb("yc", [128, 8 * 512], BF16)
        src_ = sb("src_", [128, 8 * 512], BF16)
        rwkvb = sb("rwkvb", [128, 8 * 512], BF16)
        x2b = sb("x2b", [128, 8 * 512])
        xmidb = sb("xmidb", [128, 8 * 512])
        s2m = sb("s2m", [128, 8 * 512])
        gb_ = sb("gb_", [128, 8 * 512])
        dcol = sb("dcol", [128, 8 * 512])
        gkb = sb("gkb", [128, 8 * 512], BF16)
        grb = sb("grb", [128, 8 * 512], BF16)
        sigb = sb("sigb", [128, 8 * 512], BF16)
        wo_sb = sb("wo_sb", [128, 8 * D], BF16)
        fr_sb = sb("fr_sb", [128, 8 * D], BF16)
        cv = sb("cv", [128, 32])
        maskc = sb("maskc", [128, 1])
        gprev = sb("gprev", [128, 8])
        sq512 = sb("sq512", [128, 512])
        rows = sb("rows", [1, 2 * 512])
        mrow2 = sb("mrow2", [1, 1024])
        mbc = sb("mbc", [128, 1024])
        onescol = sb("onescol", [128, 1])
        ones1 = sb("ones1", [1, 128])
        ident4 = sb("ident4", [128, 128])

        pX0 = psum("pX0", [128, 512])
        pX1 = psum("pX1", [128, 512])
        pA0 = psum("pA0", [128, 512])
        pA1 = psum("pA1", [128, 512])
        pSt = psum("pSt", [1, 1024])
        pM = psum("pM", [128, 1024])

        s_x = [st.enter_context(nc.semaphore(f"s_x{i}")) for i in range(2)]
        s_y = st.enter_context(nc.semaphore("s_y"))
        s_r = st.enter_context(nc.semaphore("s_r"))
        s_c = st.enter_context(nc.semaphore("s_c"))
        s_pe = st.enter_context(nc.semaphore("s_pe"))
        s_act = st.enter_context(nc.semaphore("s_act"))
        s_dve = st.enter_context(nc.semaphore("s_dve"))
        s_st1 = st.enter_context(nc.semaphore("s_st1"))
        s_st2 = st.enter_context(nc.semaphore("s_st2"))
        s_st3 = st.enter_context(nc.semaphore("s_st3"))
        block = st.enter_context(nc.Block())

        def emit_store(sync, B, cj):
            cjs, cjw = CH[cj]
            sync.wait_ge(s_dve, B.at(("s2m", cj)))
            sync.dma_start(out=s2m_out[:, cjs:cjs + cjw].rearrange("(j p) t -> p j t", p=128),
                           in_=s2m[:, 0:8 * cjw].rearrange("p (j t) -> p j t", t=cjw)
                           ).then_inc(s_st1, 16)
            sync.wait_ge(s_dve, B.at(("mix", cj)))
            sync.dma_start(out=gk_out[:, cjs:cjs + cjw].rearrange("(j p) t -> p j t", p=128),
                           in_=gkb[:, 0:8 * cjw].rearrange("p (j t) -> p j t", t=cjw)
                           ).then_inc(s_st2, 16)
            sync.wait_ge(s_dve, B.at(("sig", cj, 7)))
            sync.dma_start(out=sig_out[:, cjs:cjs + cjw].rearrange("(j p) t -> p j t", p=128),
                           in_=sigb[:, 0:8 * cjw].rearrange("p (j t) -> p j t", t=cjw)
                           ).then_inc(s_st3, 16)

        def f_sync(sync, B):
            for ci, (cs, cw) in enumerate(CH):
                if ci >= 2:
                    sync.wait_ge(s_dve, B.at(("xmid", ci - 2, 7)))
                sync.dma_start(out=x2c[:, (ci % 2) * 4 * D:(ci % 2) * 4 * D + (cw // 128) * D]
                               .rearrange("p (s c) -> p s c", c=D),
                               in_=x2_in[cs:cs + cw, :].rearrange("(s p) c -> p s c", p=128)
                               ).then_inc(s_x[ci % 2], 16)
                if ci >= 1:
                    sync.wait_ge(s_dve, B.at(("rwkv", ci - 1)))
                sync.dma_start(out=yc[:, 0:8 * cw]
                               .rearrange("p (j t) -> p j t", t=cw),
                               in_=y_in[:, cs:cs + cw].rearrange("(j p) t -> p j t", p=128)
                               ).then_inc(s_y, 16)
                sync.dma_start(out=src_[:, 0:8 * cw]
                               .rearrange("p (j t) -> p j t", t=cw),
                               in_=sr_in[:, cs:cs + cw].rearrange("(j p) t -> p j t", p=128)
                               ).then_inc(s_r, 16)
                if ci >= 1:
                    emit_store(sync, B, ci - 1)
            emit_store(sync, B, NCH - 1)
            sync.wait_ge(s_st1, 16 * NCH)
            sync.wait_ge(s_st2, 16 * NCH)
            sync.wait_ge(s_st3, 16 * NCH)

        def f_gpsimd(g, B):
            g.dma_start(out=wo_sb[:].rearrange("p (j c) -> p j c", c=D),
                        in_=WoT.rearrange("j p c -> p j c")).then_inc(s_c, 16)
            g.dma_start(out=fr_sb[:].rearrange("p (j c) -> p j c", c=D),
                        in_=FrT.rearrange("j p c -> p j c")).then_inc(s_c, 16)
            g.dma_start(out=cv[:], in_=c4).then_inc(s_c, 16)
            g.dma_start(out=maskc[:], in_=bmask).then_inc(s_c, 16)
            g.iota(sq512[:, 0:128], [[1, 128]], base=0, channel_multiplier=-1,
                   allow_small_or_imprecise_dtypes=True)
            g.memset(gprev[:], 0.0)
            g.memset(onescol[:], 1.0)
            g.memset(ones1[:], 1.0)
            g.memset(mrow2[:], 0.0)
            g.memset(rows[:], 0.0)
            g.drain().then_inc(s_dve, 1)
            B.tag(s_dve, "init")

        def f_vector(v, B):
            v.wait_ge(s_dve, B.at("init"))
            v.wait_ge(s_c, 64)
            v.tensor_scalar(out=ident4[:], in0=sq512[:, 0:128], scalar1=0.0,
                            scalar2=None, op0=ALU.is_equal)
            v.drain().then_inc(s_dve, 1)
            B.tag(s_dve, "ident")
            for ci, (cs, cw) in enumerate(CH):
                ns = cw // 128
                v.wait_ge(s_y, 16 * (ci + 1))
                v.wait_ge(s_r, 16 * (ci + 1))
                if ci >= 1:
                    v.wait_ge(s_pe, B.at(("att", ci - 1, 7)))
                yv = yc[:, 0:8 * cw]
                sv = src_[:, 0:8 * cw]
                v.tensor_tensor(out=rwkvb[:, 0:8 * cw], in0=yv, in1=sv, op=ALU.mult)
                v.drain().then_inc(s_dve, 1)
                B.tag(s_dve, ("rwkv", ci))
                for c8 in range(8):
                    v.wait_ge(s_pe, B.at(("att", ci, c8)))
                    v.wait_ge(s_act, B.at(("x2ev", ci, ns - 1)))
                    if ci >= 1 and c8 == 0:
                        v.wait_ge(s_pe, B.at(("colsq", ci - 1, 7)))
                    pa = pA0 if c8 % 2 == 0 else pA1
                    v.tensor_tensor(out=xmidb[:, c8 * cw:(c8 + 1) * cw],
                                    in0=x2b[:, c8 * cw:(c8 + 1) * cw], in1=pa[:, 0:cw],
                                    op=ALU.add)
                    v.drain()
                    if ci >= 1 and c8 == 0:
                        v.wait_ge(s_st1, 16 * ci)
                    v.tensor_tensor(out=s2m[:, c8 * cw:(c8 + 1) * cw],
                                    in0=x2b[:, c8 * cw:(c8 + 1) * cw],
                                    in1=xmidb[:, c8 * cw:(c8 + 1) * cw], op=ALU.add)
                    v.drain().then_inc(s_dve, 1)
                    B.tag(s_dve, ("xmid", ci, c8))
                B.tag(s_dve, ("s2m", ci))
                v.wait_ge(s_pe, B.at(("colsq", ci, 7)))
                v.tensor_copy(rows[0:1, 0:cw], pSt[0:1, 0:cw])
                v.tensor_copy(rows[0:1, 512:512 + cw], pSt[0:1, 512:512 + cw])
                v.drain()
                v.tensor_scalar(out=mrow2[0:1, 0:cw], in0=rows[0:1, 0:cw], scalar1=1.0 / D,
                                scalar2=None, op0=ALU.mult)
                v.drain()
                v.tensor_scalar(out=rows[0:1, 512:512 + cw], in0=rows[0:1, 512:512 + cw],
                                scalar1=1.0 / D, scalar2=None, op0=ALU.mult)
                v.tensor_tensor(out=rows[0:1, 0:cw], in0=mrow2[0:1, 0:cw],
                                in1=mrow2[0:1, 0:cw], op=ALU.mult)
                v.drain()
                v.tensor_tensor(out=rows[0:1, 512:512 + cw], in0=rows[0:1, 512:512 + cw],
                                in1=rows[0:1, 0:cw], op=ALU.subtract)
                v.drain()
                v.tensor_scalar(out=rows[0:1, 512:512 + cw], in0=rows[0:1, 512:512 + cw],
                                scalar1=EPS, scalar2=None, op0=ALU.add)
                v.drain()
                v.reciprocal(out=rows[0:1, 512:512 + cw], in_=rows[0:1, 512:512 + cw])
                v.drain().then_inc(s_dve, 1)
                B.tag(s_dve, ("rowvar", ci))
                v.wait_ge(s_pe, B.at(("bc", ci)))
                v.tensor_copy(mbc[:], pM[:])
                v.drain()
                mb3 = mbc[:, 0:512].rearrange("p (o t) -> p o t", o=1)[:, :, 0:cw]
                rb3 = mbc[:, 512:1024].rearrange("p (o t) -> p o t", o=1)[:, :, 0:cw]
                xb3 = xmidb[:, 0:8 * cw].rearrange("p (j t) -> p j t", t=cw)
                gb3 = gb_[:, 0:8 * cw].rearrange("p (j t) -> p j t", t=cw)
                v.tensor_tensor(out=gb3, in0=xb3, in1=bc(mb3, xb3), op=ALU.subtract)
                v.drain()
                v.tensor_tensor(out=gb3, in0=gb3, in1=bc(rb3, gb3), op=ALU.mult)
                v.drain().then_inc(s_dve, 1)
                B.tag(s_dve, ("gnorm", ci))
                v.wait_ge(s_act, B.at(("gb2", ci)))
                if ci >= 1:
                    v.wait_ge(s_pe, B.at(("fr", ci - 1, 7)))
                    v.wait_ge(s_st2, 16 * ci)
                for j in range(8):
                    hj = gb_[:, j * cw:(j + 1) * cw]
                    dj = dcol[:, j * cw:(j + 1) * cw]
                    v.tensor_tensor(out=dj[:, 1:cw], in0=hj[:, 1:cw], in1=hj[:, 0:cw - 1],
                                    op=ALU.subtract)
                    v.tensor_tensor(out=dj[:, 0:1], in0=hj[:, 0:1], in1=gprev[:, j:j + 1],
                                    op=ALU.subtract)
                v.drain()
                for mi, outb in [(2, gkb), (3, grb)]:
                    for j in range(8):
                        hj = gb_[:, j * cw:(j + 1) * cw]
                        dj = dcol[:, j * cw:(j + 1) * cw]
                        ob = outb[:, j * cw:(j + 1) * cw]
                        tm = cv[:, mi * 8 + j:mi * 8 + j + 1]
                        v.scalar_tensor_tensor(out=ob[:, 1:cw], in0=dj[:, 1:cw], scalar=tm,
                                               in1=hj[:, 0:cw - 1], op0=ALU.mult, op1=ALU.add)
                        v.scalar_tensor_tensor(out=ob[:, 0:1], in0=dj[:, 0:1], scalar=tm,
                                               in1=gprev[:, j:j + 1], op0=ALU.mult, op1=ALU.add)
                v.drain()
                for j in range(8):
                    v.tensor_copy(gprev[:, j:j + 1], gb_[:, j * cw + cw - 1:j * cw + cw])
                v.drain()
                if ci == 0 and EX:
                    v.tensor_scalar(out=gprev[:], in0=gprev[:], scalar1=maskc[:],
                                    scalar2=None, op0=ALU.mult)
                    v.drain()
                v.engine_nop().then_inc(s_dve, 1)
                B.tag(s_dve, ("mix", ci))
                for d8 in range(8):
                    v.wait_ge(s_act, B.at(("frt", ci, d8)))
                    if ci >= 1 and d8 == 0:
                        v.wait_ge(s_st3, 16 * ci)
                    v.tensor_scalar(out=sigb[:, d8 * cw:(d8 + 1) * cw], in0=dcol[:, 0:cw],
                                    scalar1=0.5, scalar2=0.5, op0=ALU.mult, op1=ALU.add)
                    v.drain().then_inc(s_dve, 1)
                    B.tag(s_dve, ("sig", ci, d8))

        def f_scalar(sc, B):
            for ci, (cs, cw) in enumerate(CH):
                ns = cw // 128
                for s in range(ns):
                    sc.wait_ge(s_pe, B.at(("xT", ci, s)))
                    for j in range(8):
                        bank = pX0 if j < 4 else pX1
                        slot = (j % 4) * 128
                        sc.copy(out=x2b[:, j * cw + s * 128:j * cw + s * 128 + 128],
                                in_=bank[:, slot:slot + 128])
                    sc.drain().then_inc(s_act, 1)
                    B.tag(s_act, ("x2ev", ci, s))
                for c8 in range(8):
                    sc.wait_ge(s_dve, B.at(("xmid", ci, c8)))
                    if ci >= 1 or c8 >= 1:
                        prev = (ci, c8 - 1) if c8 > 0 else (ci - 1, 7)
                        sc.wait_ge(s_pe, B.at(("colsq",) + prev))
                    sc.activation(out=sq512[:, 0:cw], in_=xmidb[:, c8 * cw:(c8 + 1) * cw],
                                  func=AF.Square)
                    sc.drain().then_inc(s_act, 1)
                    B.tag(s_act, ("sq", ci, c8))
                sc.wait_ge(s_dve, B.at(("rowvar", ci)))
                sc.activation(out=mrow2[0:1, 512:512 + cw], in_=rows[0:1, 512:512 + cw],
                              func=AF.Ln)
                sc.drain()
                sc.activation(out=mrow2[0:1, 512:512 + cw], in_=mrow2[0:1, 512:512 + cw],
                              func=AF.Exp, scale=0.5)
                sc.drain().then_inc(s_act, 1)
                B.tag(s_act, ("rstdrow", ci))
                sc.wait_ge(s_dve, B.at(("gnorm", ci)))
                for j in range(8):
                    hj = gb_[:, j * cw:(j + 1) * cw]
                    sc.activation(out=hj, in_=hj, func=AF.Identity,
                                  scale=cv[:, j:j + 1], bias=cv[:, 8 + j:8 + j + 1])
                    sc.drain()
                sc.drain().then_inc(s_act, 1)
                B.tag(s_act, ("gb2", ci))
                for d8 in range(8):
                    sc.wait_ge(s_pe, B.at(("fr", ci, d8)))
                    prevd = (ci, d8 - 1) if d8 > 0 else ((ci - 1, 7) if ci > 0 else None)
                    if prevd is not None:
                        sc.wait_ge(s_dve, B.at(("sig",) + prevd))
                    sc.activation(out=dcol[:, 0:cw], in_=pA0[:, 0:cw], func=AF.Tanh, scale=0.5)
                    sc.drain().then_inc(s_act, 1)
                    B.tag(s_act, ("frt", ci, d8))

        def f_tensor(te, B):
            te.wait_ge(s_c, 64)
            te.wait_ge(s_dve, B.at("ident"))
            for ci, (cs, cw) in enumerate(CH):
                ns = cw // 128
                xoff = (ci % 2) * 4 * D
                te.wait_ge(s_x[ci % 2], 16 * (ci // 2 + 1))
                for s in range(ns):
                    if ci > 0 or s > 0:
                        prev = (ci, s - 1) if s > 0 else (ci - 1, CH[ci - 1][1] // 128 - 1)
                        te.wait_ge(s_act, B.at(("x2ev",) + prev))
                    if ci >= 1 and s == 0:
                        te.wait_ge(s_dve, B.at(("gnorm", ci - 1)))
                    last = None
                    for j in range(8):
                        bank = pX0 if j < 4 else pX1
                        slot = (j % 4) * 128
                        last = te.transpose(bank[:, slot:slot + 128],
                                            x2c[:, xoff + s * D + j * 128:xoff + s * D + (j + 1) * 128],
                                            ident4[:])
                    last.then_inc(s_pe, 1)
                    B.tag(s_pe, ("xT", ci, s))
                te.wait_ge(s_dve, B.at(("rwkv", ci)))
                te.wait_ge(s_act, B.at(("x2ev", ci, ns - 1)))
                for c8 in range(8):
                    pa = pA0 if c8 % 2 == 0 else pA1
                    if ci >= 1 and c8 == 0:
                        te.wait_ge(s_act, B.at(("frt", ci - 1, 7)))
                    if ci > 0 or c8 > 1:
                        prev = (ci, c8 - 2) if c8 > 1 else (ci - 1, 6 + c8)
                        te.wait_ge(s_dve, B.at(("xmid",) + prev))
                    last = None
                    for dj in range(8):
                        last = te.matmul(pa[:, 0:cw],
                                         wo_sb[:, dj * D + c8 * 128:dj * D + c8 * 128 + 128],
                                         rwkvb[:, dj * cw:(dj + 1) * cw],
                                         start=(dj == 0), stop=(dj == 7))
                    last.then_inc(s_pe, 1)
                    B.tag(s_pe, ("att", ci, c8))
                for c8 in range(8):
                    te.wait_ge(s_dve, B.at(("xmid", ci, c8)))
                    te.wait_ge(s_act, B.at(("sq", ci, c8)))
                    if ci >= 1 and c8 == 0:
                        te.wait_ge(s_dve, B.at(("rowvar", ci - 1)))
                    te.matmul(pSt[0:1, 0:cw], onescol[:], xmidb[:, c8 * cw:(c8 + 1) * cw],
                              start=(c8 == 0), stop=(c8 == 7))
                    last = te.matmul(pSt[0:1, 512:512 + cw], onescol[:], sq512[:, 0:cw],
                                     start=(c8 == 0), stop=(c8 == 7))
                    last.then_inc(s_pe, 1)
                    B.tag(s_pe, ("colsq", ci, c8))
                te.wait_ge(s_act, B.at(("rstdrow", ci)))
                if ci >= 1:
                    te.wait_ge(s_dve, B.at(("gnorm", ci - 1)))
                te.matmul(pM[:, 0:512], ones1[:], mrow2[0:1, 0:512], start=True, stop=True)
                last = te.matmul(pM[:, 512:1024], ones1[:], mrow2[0:1, 512:1024],
                                 start=True, stop=True)
                last.then_inc(s_pe, 1)
                B.tag(s_pe, ("bc", ci))
                te.wait_ge(s_dve, B.at(("mix", ci)))
                for d8 in range(8):
                    prevd = (ci, d8 - 1) if d8 > 0 else ((ci - 1, 7) if ci > 0 else None)
                    if prevd is not None:
                        te.wait_ge(s_act, B.at(("frt",) + prevd))
                    last = None
                    for j in range(8):
                        last = te.matmul(pA0[:, 0:cw],
                                         fr_sb[:, j * D + d8 * 128:j * D + d8 * 128 + 128],
                                         grb[:, j * cw:(j + 1) * cw],
                                         start=(j == 0), stop=(j == 7))
                    last.then_inc(s_pe, 1)
                    B.tag(s_pe, ("fr", ci, d8))

        two_pass(block, None, [("sync", f_sync), ("gpsimd", f_gpsimd),
                               ("vector", f_vector), ("scalar", f_scalar),
                               ("tensor", f_tensor)])
    return nc


def stageC3a_inputs(inp, x2_glob, y_glob, sr_core, T, cid):
    b, t0, T_loc = shard_tokens(T, cid)
    lo = t0 - EX
    clo = max(lo, 0)
    x2s = np.zeros((T_loc + EX, D), np.float32)
    x2s[clo - lo:] = x2_glob[b, clo:t0 + T_loc]
    import ml_dtypes
    ys = np.zeros((D, T_loc + EX), ml_dtypes.bfloat16)
    ys[:, clo - lo:] = y_glob[b, :, clo:t0 + T_loc]

    def col8(vec):
        return np.ascontiguousarray(vec.reshape(8, 128).T)

    c4v = np.concatenate([col8(inp["L1_attn_ln2_g"]), col8(inp["L1_attn_ln2_b"]),
                          col8(inp["L1_attn_fm_k"]), col8(inp["L1_attn_fm_r"])], axis=1)
    msk = np.full((128, 1), 0.0 if cid % 4 == 0 else 1.0, np.float32)
    return {"x2_in": x2s, "y_in": ys, "sr_in": sr_core,
            "WoT": np.ascontiguousarray(inp["L1_attn_Wo"].T.reshape(8, 128, D)),
            "FrT": np.ascontiguousarray(inp["L1_attn_Fr"].T.reshape(8, 128, D)),
            "c4": np.ascontiguousarray(c4v), "bmask": msk}


def build_stageC3b(T_loc):
    F1 = 4096
    CH = [(s, 512) for s in range(0, T_loc, 512)]
    NCH = len(CH)

    nc = bass.Bass("TRN2", target_bir_lowering=False)
    gk_in = nc.dram_tensor("gk_in", [D, T_loc], BF16, kind="ExternalInput").ap()
    sig_in = nc.dram_tensor("sig_in", [D, T_loc], BF16, kind="ExternalInput").ap()
    s2m_in = nc.dram_tensor("s2m_in", [D, T_loc], F32, kind="ExternalInput").ap()
    FkT = nc.dram_tensor("FkT", [8, 128, F1], F32, kind="ExternalInput").ap()
    FvT = nc.dram_tensor("FvT", [32, 128, D], F32, kind="ExternalInput").ap()
    x3a_out = nc.dram_tensor("x3a_out", [T_loc, D], F32, kind="ExternalOutput").ap()

    with ExitStack() as st:
        def sb(name, shape, dt=F32):
            return st.enter_context(nc.sbuf_tensor(name, shape, dt))

        def psum(name, shape, dt=F32):
            return st.enter_context(nc.psum_tensor(name, shape, dt))

        fk_sb = sb("fk_sb", [128, 8 * F1], BF16)
        fv_sb = sb("fv_sb", [128, 32 * D], BF16)
        gkc = sb("gkc", [128, 4096], BF16)
        sgc = sb("sgc", [128, 4096], BF16)
        smc = sb("smc", [128, 4096])
        kfq = sb("kfq", [128, 32 * 512], BF16)
        sq512 = sb("sq512", [128, 512])
        mrow2 = sb("mrow2", [1, 1024])
        mbc = sb("mbc", [128, 1024])
        rows = mbc[0:1, :]
        x3a = sb("x3a", [128, D])
        x3p = smc
        onescol = sb("onescol", [128, 1])
        ones1 = sb("ones1", [1, 128])
        ident5 = sb("ident5", [128, 128])

        pKf = psum("pKf", [128, 512])
        pKv0 = psum("pKv0", [128, 512])
        pKv1 = psum("pKv1", [128, 512])
        pSt = psum("pSt", [1, 512])
        pM = psum("pM", [128, 1024])
        pX0 = psum("pX0", [128, 512])
        pX1 = psum("pX1", [128, 512])

        s_g = st.enter_context(nc.semaphore("s_g"))
        s_s = st.enter_context(nc.semaphore("s_s"))
        s_m = st.enter_context(nc.semaphore("s_m"))
        s_c = st.enter_context(nc.semaphore("s_c"))
        s_pe = st.enter_context(nc.semaphore("s_pe"))
        s_act = st.enter_context(nc.semaphore("s_act"))
        s_dve = st.enter_context(nc.semaphore("s_dve"))
        s_st = st.enter_context(nc.semaphore("s_st"))
        block = st.enter_context(nc.Block())

        def f_sync(sync, B):
            for ci, (cs, cw) in enumerate(CH):
                if ci >= 1:
                    sync.wait_ge(s_pe, B.at(("kf", ci - 1, 31)))
                sync.dma_start(out=gkc[:, 0:8 * cw]
                               .rearrange("p (j t) -> p j t", t=cw),
                               in_=gk_in[:, cs:cs + cw].rearrange("(j p) t -> p j t", p=128)
                               ).then_inc(s_g, 16)
                if ci >= 1:
                    sync.wait_ge(s_dve, B.at(("x3p", ci - 1, 7)))
                    sync.wait_ge(s_pe, B.at(("x3T", ci - 1, 3)))
                sync.dma_start(out=sgc[:, 0:8 * cw].rearrange("p (j t) -> p j t", t=cw),
                               in_=sig_in[:, cs:cs + cw].rearrange("(j p) t -> p j t", p=128)
                               ).then_inc(s_s, 16)
                sync.dma_start(out=smc[:, 0:8 * cw].rearrange("p (j t) -> p j t", t=cw),
                               in_=s2m_in[:, cs:cs + cw].rearrange("(j p) t -> p j t", p=128)
                               ).then_inc(s_m, 16)
                for s in range(4):
                    cj, sj = (ci, s)
                    sync.wait_ge(s_act, B.at(("x3ev", cj, sj)))
                    sync.dma_start(out=x3a_out[cs + sj * 128:cs + sj * 128 + 128, :],
                                   in_=x3a[:]).then_inc(s_st, 16)
            sync.wait_ge(s_st, 16 * 4 * NCH)

        def f_gpsimd(g, B):
            g.dma_start(out=fk_sb[:].rearrange("p (j f) -> p j f", f=F1),
                        in_=FkT.rearrange("j p f -> p j f")).then_inc(s_c, 16)
            g.dma_start(out=fv_sb[:].rearrange("p (j c) -> p j c", c=D),
                        in_=FvT.rearrange("j p c -> p j c")).then_inc(s_c, 16)
            g.iota(sq512[:, 0:128], [[1, 128]], base=0, channel_multiplier=-1,
                   allow_small_or_imprecise_dtypes=True)
            g.memset(onescol[:], 1.0)
            g.memset(ones1[:], 1.0)
            g.memset(mrow2[:], 0.0)
            g.memset(mbc[:], 0.0)
            g.drain().then_inc(s_dve, 1)
            B.tag(s_dve, "init")

        def f_vector(v, B):
            v.wait_ge(s_dve, B.at("init"))
            v.wait_ge(s_c, 32)
            v.tensor_scalar(out=ident5[:], in0=sq512[:, 0:128], scalar1=0.0,
                            scalar2=None, op0=ALU.is_equal)
            v.drain().then_inc(s_dve, 1)
            B.tag(s_dve, "ident")
            for ci, (cs, cw) in enumerate(CH):
                for fj in range(32):
                    v.wait_ge(s_act, B.at(("rlu", ci, fj)))
                    if ci >= 1 and fj == 0:
                        v.wait_ge(s_pe, B.at(("kv", ci - 1, 7)))
                    v.tensor_tensor(out=kfq[:, fj * cw:(fj + 1) * cw], in0=x3a[:, 0:cw],
                                    in1=x3a[:, 0:cw], op=ALU.mult)
                    v.drain().then_inc(s_dve, 1)
                    B.tag(s_dve, ("kfq", ci, fj))
                v.wait_ge(s_s, 16 * (ci + 1))
                v.wait_ge(s_m, 16 * (ci + 1))
                for c8 in range(8):
                    v.wait_ge(s_pe, B.at(("kv", ci, c8)))
                    if ci >= 1 and c8 == 0:
                        v.wait_ge(s_pe, B.at(("x3T", ci - 1, 3)))
                        v.wait_ge(s_st, 16 * 4 * ci)
                    pkv = pKv0 if c8 % 2 == 0 else pKv1
                    tmp = x3a[:, 512:512 + cw]
                    v.tensor_tensor(out=tmp, in0=sgc[:, c8 * cw:(c8 + 1) * cw],
                                    in1=pkv[:, 0:cw], op=ALU.mult)
                    v.drain()
                    v.tensor_tensor(out=x3p[:, c8 * cw:(c8 + 1) * cw],
                                    in0=x3p[:, c8 * cw:(c8 + 1) * cw],
                                    in1=tmp, op=ALU.add)
                    v.drain().then_inc(s_dve, 1)
                    B.tag(s_dve, ("x3p", ci, c8))
                v.wait_ge(s_pe, B.at(("colsq", ci, 7)))
                v.tensor_copy(rows[0:1, 0:cw], pSt[0:1, 0:cw])
                v.tensor_copy(rows[0:1, 512:512 + cw], pM[0:1, 0:cw])
                v.drain()
                v.tensor_scalar(out=mrow2[0:1, 0:cw], in0=rows[0:1, 0:cw], scalar1=1.0 / D,
                                scalar2=None, op0=ALU.mult)
                v.drain()
                v.tensor_scalar(out=rows[0:1, 512:512 + cw], in0=rows[0:1, 512:512 + cw],
                                scalar1=1.0 / D, scalar2=None, op0=ALU.mult)
                v.tensor_tensor(out=rows[0:1, 0:cw], in0=mrow2[0:1, 0:cw],
                                in1=mrow2[0:1, 0:cw], op=ALU.mult)
                v.drain()
                v.tensor_tensor(out=rows[0:1, 512:512 + cw], in0=rows[0:1, 512:512 + cw],
                                in1=rows[0:1, 0:cw], op=ALU.subtract)
                v.drain()
                v.tensor_scalar(out=rows[0:1, 512:512 + cw], in0=rows[0:1, 512:512 + cw],
                                scalar1=EPS, scalar2=None, op0=ALU.add)
                v.drain()
                v.reciprocal(out=rows[0:1, 512:512 + cw], in_=rows[0:1, 512:512 + cw])
                v.drain().then_inc(s_dve, 1)
                B.tag(s_dve, ("rowvar", ci))
                v.wait_ge(s_pe, B.at(("bc", ci)))
                v.tensor_copy(mbc[:], pM[:])
                v.drain()
                mb3 = mbc[:, 0:512].rearrange("p (o t) -> p o t", o=1)[:, :, 0:cw]
                rb3 = mbc[:, 512:1024].rearrange("p (o t) -> p o t", o=1)[:, :, 0:cw]
                xb3 = x3p[:, 0:8 * cw].rearrange("p (j t) -> p j t", t=cw)
                v.tensor_tensor(out=xb3, in0=xb3, in1=bc(mb3, xb3), op=ALU.subtract)
                v.drain()
                v.tensor_tensor(out=xb3, in0=xb3, in1=bc(rb3, xb3), op=ALU.mult)
                v.drain().then_inc(s_dve, 1)
                B.tag(s_dve, ("x3n", ci))

        def f_scalar(sc, B):
            for ci, (cs, cw) in enumerate(CH):
                for fj in range(32):
                    sc.wait_ge(s_pe, B.at(("kf", ci, fj)))
                    prevf = (ci, fj - 1) if fj > 0 else ((ci - 1, 31) if ci > 0 else None)
                    if prevf is not None:
                        sc.wait_ge(s_dve, B.at(("kfq",) + prevf))
                    if ci >= 1 and fj == 0:
                        sc.wait_ge(s_st, 16 * 4 * ci)
                    sc.activation(out=x3a[:, 0:cw], in_=pKf[:, 0:cw], func=AF.Relu)
                    sc.drain().then_inc(s_act, 1)
                    B.tag(s_act, ("rlu", ci, fj))
                for c8 in range(8):
                    sc.wait_ge(s_dve, B.at(("x3p", ci, c8)))
                    if ci >= 1 or c8 >= 1:
                        prev = (ci, c8 - 1) if c8 > 0 else (ci - 1, 7)
                        sc.wait_ge(s_pe, B.at(("colsq",) + prev))
                    sc.activation(out=sq512[:, 0:cw], in_=x3p[:, c8 * cw:(c8 + 1) * cw],
                                  func=AF.Square)
                    sc.drain().then_inc(s_act, 1)
                    B.tag(s_act, ("sq", ci, c8))
                sc.wait_ge(s_dve, B.at(("rowvar", ci)))
                sc.activation(out=mrow2[0:1, 512:512 + cw], in_=rows[0:1, 512:512 + cw],
                              func=AF.Ln)
                sc.drain()
                sc.activation(out=mrow2[0:1, 512:512 + cw], in_=mrow2[0:1, 512:512 + cw],
                              func=AF.Exp, scale=0.5)
                sc.drain().then_inc(s_act, 1)
                B.tag(s_act, ("rstdrow", ci))
                for s in range(4):
                    sc.wait_ge(s_pe, B.at(("x3T", ci, s)))
                    if ci > 0 or s > 0:
                        sc.wait_ge(s_st, 16 * (4 * ci + s))
                    for j in range(8):
                        bank = pX0 if j < 4 else pX1
                        slot = (j % 4) * 128
                        sc.copy(out=x3a[:, j * 128:(j + 1) * 128], in_=bank[:, slot:slot + 128])
                    sc.drain().then_inc(s_act, 1)
                    B.tag(s_act, ("x3ev", ci, s))

        def f_tensor(te, B):
            te.wait_ge(s_c, 32)
            te.wait_ge(s_dve, B.at("ident"))
            for ci, (cs, cw) in enumerate(CH):
                te.wait_ge(s_g, 16 * (ci + 1))
                for fj in range(32):
                    if ci > 0 or fj > 0:
                        prevf = (ci, fj - 1) if fj > 0 else (ci - 1, 31)
                        te.wait_ge(s_act, B.at(("rlu",) + prevf))
                    last = None
                    for j in range(8):
                        last = te.matmul(pKf[:, 0:cw],
                                         fk_sb[:, j * F1 + fj * 128:j * F1 + fj * 128 + 128],
                                         gkc[:, j * cw:(j + 1) * cw],
                                         start=(j == 0), stop=(j == 7))
                    last.then_inc(s_pe, 1)
                    B.tag(s_pe, ("kf", ci, fj))
                te.wait_ge(s_dve, B.at(("kfq", ci, 31)))
                for c8 in range(8):
                    pkv = pKv0 if c8 % 2 == 0 else pKv1
                    if ci > 0 or c8 > 1:
                        prev = (ci, c8 - 2) if c8 > 1 else (ci - 1, 6 + c8)
                        te.wait_ge(s_dve, B.at(("x3p",) + prev))
                    last = None
                    for fj in range(32):
                        last = te.matmul(pkv[:, 0:cw],
                                         fv_sb[:, fj * D + c8 * 128:fj * D + c8 * 128 + 128],
                                         kfq[:, fj * cw:(fj + 1) * cw],
                                         start=(fj == 0), stop=(fj == 31))
                    last.then_inc(s_pe, 1)
                    B.tag(s_pe, ("kv", ci, c8))
                for c8 in range(8):
                    te.wait_ge(s_dve, B.at(("x3p", ci, c8)))
                    te.wait_ge(s_act, B.at(("sq", ci, c8)))
                    if ci >= 1 and c8 == 0:
                        te.wait_ge(s_dve, B.at(("x3n", ci - 1)))
                    te.matmul(pSt[0:1, 0:cw], onescol[:], x3p[:, c8 * cw:(c8 + 1) * cw],
                              start=(c8 == 0), stop=(c8 == 7))
                    last = te.matmul(pM[0:1, 0:cw], onescol[:], sq512[:, 0:cw],
                                     start=(c8 == 0), stop=(c8 == 7))
                    last.then_inc(s_pe, 1)
                    B.tag(s_pe, ("colsq", ci, c8))
                te.wait_ge(s_act, B.at(("rstdrow", ci)))
                if ci >= 1:
                    te.wait_ge(s_dve, B.at(("x3n", ci - 1)))
                te.matmul(pM[:, 0:512], ones1[:], mrow2[0:1, 0:512], start=True, stop=True)
                last = te.matmul(pM[:, 512:1024], ones1[:], mrow2[0:1, 512:1024],
                                 start=True, stop=True)
                last.then_inc(s_pe, 1)
                B.tag(s_pe, ("bc", ci))
                te.wait_ge(s_dve, B.at(("x3n", ci)))
                for s in range(4):
                    if ci > 0 or s > 0:
                        prev = (ci, s - 1) if s > 0 else (ci - 1, 3)
                        te.wait_ge(s_act, B.at(("x3ev",) + prev))
                    last = None
                    for j in range(8):
                        bank = pX0 if j < 4 else pX1
                        slot = (j % 4) * 128
                        last = te.transpose(bank[:, slot:slot + 128],
                                            x3p[:, j * cw + s * 128:j * cw + s * 128 + 128],
                                            ident5[:])
                    last.then_inc(s_pe, 1)
                    B.tag(s_pe, ("x3T", ci, s))

        two_pass(block, None, [("sync", f_sync), ("gpsimd", f_gpsimd),
                               ("vector", f_vector), ("scalar", f_scalar),
                               ("tensor", f_tensor)])
    return nc


def stageC3b_inputs(inp, gk_core, sig_core, s2m_core, T, cid):
    F1 = 4096
    return {"gk_in": gk_core, "sig_in": sig_core, "s2m_in": s2m_core,
            "FkT": np.ascontiguousarray(inp["L1_attn_Fk"].T.reshape(8, 128, F1)),
            "FvT": np.ascontiguousarray(inp["L1_attn_Fv"].T.reshape(32, 128, D))}


def emit_fp6_ladder(v, width, au_ap, sg_ap, out_ap, scr, out_wait=None):
    nb = width // 32
    un = scr["un"][:, 0:width]
    u0 = scr["lad"][:, 0:width]
    fb = scr["lad2"][:, 0:width]
    bm = scr["bm"][:, 0:nb]
    rb = scr["rb6"][:, 0:nb]
    sc = scr["sc6"][:, 0:nb]
    au3 = au_ap.rearrange("p (nb w) -> p nb w", w=32)
    v.tensor_reduce(out=bm, in_=au3, axis=AX.X, op=ALU.max)
    v.drain()
    v.tensor_scalar(out=bm, in0=bm, scalar1=7.5e-30, scalar2=None, op0=ALU.max)
    v.drain()
    v.reciprocal(out=rb, in_=bm)
    v.tensor_scalar(out=sc, in0=bm, scalar1=1.0 / 7.5, scalar2=None, op0=ALU.mult)
    v.drain()
    v.tensor_scalar(out=rb, in0=rb, scalar1=7.5, scalar2=None, op0=ALU.mult)
    v.drain()
    un3 = un.rearrange("p (nb w) -> p nb w", w=32)
    rb3 = rb.rearrange("p (nb w) -> p nb w", w=1)
    v.tensor_tensor(out=un3, in0=au3, in1=bc(rb3, au3), op=ALU.mult)
    v.drain()
    v.tensor_scalar(out=fb, in0=un, scalar1=1.9375, scalar2=-4.0, op0=ALU.is_gt, op1=ALU.mult)
    v.drain()
    v.tensor_scalar(out=fb, in0=fb, scalar1=8.0, scalar2=None, op0=ALU.add)
    v.drain()
    v.tensor_scalar(out=u0, in0=un, scalar1=3.875, scalar2=None, op0=ALU.is_gt)
    v.drain()
    v.scalar_tensor_tensor(out=fb, in0=u0, scalar=-2.0, in1=fb, op0=ALU.mult, op1=ALU.add)
    v.drain()
    v.tensor_tensor(out=un, in0=un, in1=fb, op=ALU.mult)
    v.drain()
    v.tensor_scalar(out=un, in0=un, scalar1=0.5, scalar2=None, op0=ALU.add)
    v.drain()
    v.tensor_copy(u0, un)
    v.drain()
    for K in (8.0, 4.0, 2.0, 1.0):
        v.tensor_scalar(out=scr["stage"][:, 0:width], in0=un, scalar1=K, scalar2=K,
                        op0=ALU.is_ge, op1=ALU.mult)
        v.drain()
        v.tensor_tensor(out=un, in0=un, in1=scr["stage"][:, 0:width], op=ALU.subtract)
        v.drain()
    v.tensor_tensor(out=u0, in0=u0, in1=un, op=ALU.subtract)
    v.drain()
    v.reciprocal(out=fb, in_=fb)
    v.drain()
    v.tensor_tensor(out=u0, in0=u0, in1=fb, op=ALU.mult)
    v.drain()
    u03 = u0.rearrange("p (nb w) -> p nb w", w=32)
    sc3 = sc.rearrange("p (nb w) -> p nb w", w=1)
    v.tensor_tensor(out=u03, in0=u03, in1=bc(sc3, u03), op=ALU.mult)
    v.drain()
    if out_wait is not None:
        out_wait()
    v.tensor_tensor(out=out_ap, in0=u0, in1=sg_ap, op=ALU.mult)
    v.drain()


def build_stageD(T_loc):
    CH = [(s, 512) for s in range(0, T_loc, 512)]
    NCH = len(CH)

    nc = bass.Bass("TRN2", target_bir_lowering=False)
    x3_in = nc.dram_tensor("x3_in", [T_loc, D], F32, kind="ExternalInput").ap()
    weight = nc.dram_tensor("weight", [D, D], F32, kind="ExternalInput").ap()
    bias_row = nc.dram_tensor("bias_row", [1, D], F32, kind="ExternalInput").ap()
    x4_out = nc.dram_tensor("x4_out", [T_loc, D], F32, kind="ExternalOutput").ap()

    with ExitStack() as st:
        def sb(name, shape, dt=F32):
            return st.enter_context(nc.sbuf_tensor(name, shape, dt))

        def psum(name, shape, dt=F32):
            return st.enter_context(nc.psum_tensor(name, shape, dt))

        x3c = sb("x3c", [128, 2 * 4 * D])
        w6T = sb("w6T", [128, 8 * D], BF16)
        wload = sb("wload", [128, D])
        qw = sb("qw", [128, D], BF16)
        q6a = sb("q6a", [128, 4 * D], BF16)
        q6b = sb("q6b", [128, 8 * 128], BF16)
        auq = sb("auq", [128, D])
        sgq = sb("sgq", [128, D])
        unq = sb("unq", [128, D])
        ladq = sb("ladq", [128, D])
        lad2q = sb("lad2q", [128, D])
        stq = sb("stq", [128, D])
        bmq = sb("bmq", [128, 32])
        rb6q = sb("rb6q", [128, 32])
        sc6q = sb("sc6q", [128, 32])
        biasr = sb("biasr", [1, D])
        biasbc = sb("biasbc", [128, D])
        zpre4 = sb("zpre4", [128, D])
        x4n = sb("x4n", [128, D])
        dump6 = sb("dump6", [128, D])
        stat6 = sb("stat6", [128, 16])
        ident6 = sb("ident6", [128, 128], BF16)
        ones1 = sb("ones1", [1, 128])
        scr = dict(un=unq, lad=ladq, lad2=lad2q, stage=stq, bm=bmq, rb6=rb6q, sc6=sc6q)

        pT0 = psum("pT0", [128, 512], BF16)
        pT1 = psum("pT1", [128, 512], BF16)
        pO0 = psum("pO0", [128, 512])
        pO1 = psum("pO1", [128, 512])
        pB = psum("pB", [128, 1024])

        s_x = [st.enter_context(nc.semaphore(f"s_x{i}")) for i in range(2)]
        s_w = st.enter_context(nc.semaphore("s_w"))
        s_c = st.enter_context(nc.semaphore("s_c"))
        s_pe = st.enter_context(nc.semaphore("s_pe"))
        s_act = st.enter_context(nc.semaphore("s_act"))
        s_dve = st.enter_context(nc.semaphore("s_dve"))
        s_st = st.enter_context(nc.semaphore("s_st"))
        block = st.enter_context(nc.Block())

        jobs = list(range(8))

        def f_sync(sync, B):
            for wi in jobs:
                if wi >= 1:
                    sync.wait_ge(s_dve, B.at(("wq", wi - 1)))
                sync.dma_start(out=wload[:], in_=weight[wi * 128:(wi + 1) * 128, :]
                               ).then_inc(s_w, 16)
            for ci, (cs, cw) in enumerate(CH):
                if ci >= 2:
                    sync.wait_ge(s_dve, B.at(("x4", ci - 2, 3)))
                sync.dma_start(out=x3c[:, (ci % 2) * 4 * D:(ci % 2 + 1) * 4 * D]
                               .rearrange("p (s c) -> p s c", c=D),
                               in_=x3_in[cs:cs + cw, :].rearrange("(s p) c -> p s c", p=128)
                               ).then_inc(s_x[ci % 2], 16)
                for s in range(4):
                    cj, sj = ci, s
                    sync.wait_ge(s_dve, B.at(("x4", cj, sj)))
                    sync.dma_start(out=x4_out[cs + sj * 128:cs + sj * 128 + 128, :],
                                   in_=x4n[:]).then_inc(s_st, 16)
            sync.wait_ge(s_st, 16 * 4 * NCH)

        def f_gpsimd(g, B):
            g.dma_start(out=biasr[:], in_=bias_row).then_inc(s_c, 16)
            g.iota(dump6[:, 0:128], [[1, 128]], base=0, channel_multiplier=-1,
                   allow_small_or_imprecise_dtypes=True)
            g.memset(ones1[:], 1.0)
            g.drain().then_inc(s_dve, 1)
            B.tag(s_dve, "init")

        def f_vector(v, B):
            v.wait_ge(s_dve, B.at("init"))
            v.wait_ge(s_c, 16)
            v.tensor_scalar(out=dump6[:, 128:256], in0=dump6[:, 0:128], scalar1=0.0,
                            scalar2=None, op0=ALU.is_equal)
            v.drain()
            v.tensor_copy(ident6[:], dump6[:, 128:256])
            v.drain().then_inc(s_dve, 1)
            B.tag(s_dve, "ident")
            for wi in jobs:
                v.wait_ge(s_act, B.at(("wabs", wi)))
                if wi >= 1:
                    v.wait_ge(s_pe, B.at(("wT", wi - 1)))
                emit_fp6_ladder(v, D, auq[:], sgq[:], qw[:], scr)
                v.drain().then_inc(s_dve, 1)
                B.tag(s_dve, ("wq", wi))
            for ci, (cs, cw) in enumerate(CH):
                for s in range(4):
                    v.wait_ge(s_act, B.at(("xas", ci, s)))
                    if (ci, s) != (0, 0):
                        prev = (ci, s - 1) if s > 0 else (ci - 1, 3)
                        v.wait_ge(s_pe, B.at(("qT",) + prev))
                    emit_fp6_ladder(v, D, auq[:], sgq[:], q6a[:, s * D:(s + 1) * D], scr)
                    v.drain().then_inc(s_dve, 1)
                    B.tag(s_dve, ("q6", ci, s))
                    v.wait_ge(s_pe, B.at(("out1", ci, s)))
                    if (ci, s) != (0, 0):
                        prevz = (ci, s - 1) if s > 0 else (ci - 1, 3)
                        v.wait_ge(s_act, B.at(("sq6",) + prevz))
                    xa = x3c[:, (ci % 2) * 4 * D + s * D:(ci % 2) * 4 * D + (s + 1) * D]
                    v.tensor_tensor(out=zpre4[:, 0:512], in0=xa[:, 0:512], in1=pO0[:],
                                    op=ALU.add)
                    v.tensor_tensor(out=zpre4[:, 512:1024], in0=xa[:, 512:1024], in1=pO1[:],
                                    op=ALU.add)
                    v.drain()
                    v.tensor_tensor(out=zpre4[:], in0=zpre4[:], in1=biasbc[:], op=ALU.add)
                    v.drain()
                    v.tensor_reduce(out=stat6[:, 0:1], in_=zpre4[:], axis=AX.X, op=ALU.add)
                    v.drain()
                    v.tensor_scalar(out=stat6[:, 6:7], in0=stat6[:, 0:1], scalar1=1.0 / D,
                                    scalar2=None, op0=ALU.mult)
                    v.drain().then_inc(s_dve, 1)
                    B.tag(s_dve, ("zpre", ci, s))
                    v.wait_ge(s_act, B.at(("sq6", ci, s)))
                    v.tensor_scalar(out=stat6[:, 3:4], in0=stat6[:, 2:3], scalar1=1.0 / D,
                                    scalar2=None, op0=ALU.mult)
                    v.tensor_tensor(out=stat6[:, 7:8], in0=stat6[:, 6:7], in1=stat6[:, 6:7],
                                    op=ALU.mult)
                    v.drain()
                    v.tensor_tensor(out=stat6[:, 3:4], in0=stat6[:, 3:4], in1=stat6[:, 7:8],
                                    op=ALU.subtract)
                    v.drain()
                    v.tensor_scalar(out=stat6[:, 3:4], in0=stat6[:, 3:4], scalar1=EPS,
                                    scalar2=None, op0=ALU.add)
                    v.drain()
                    v.reciprocal(out=stat6[:, 3:4], in_=stat6[:, 3:4])
                    v.drain().then_inc(s_dve, 1)
                    B.tag(s_dve, ("var", ci, s))
                    v.wait_ge(s_act, B.at(("rstd6", ci, s)))
                    v.wait_ge(s_st, 16 * (4 * ci + s))
                    v.tensor_scalar(out=x4n[:], in0=zpre4[:], scalar1=stat6[:, 6:7],
                                    scalar2=stat6[:, 5:6], op0=ALU.subtract, op1=ALU.mult)
                    v.drain().then_inc(s_dve, 1)
                    B.tag(s_dve, ("x4", ci, s))

        def f_scalar(sc, B):
            sc.wait_ge(s_pe, B.at("biasbc"))
            sc.copy(out=biasbc[:], in_=pB[:, 0:1024])
            sc.drain().then_inc(s_act, 1)
            B.tag(s_act, "biasev")
            for wi in jobs:
                sc.wait_ge(s_w, 16 * (wi + 1))
                if wi >= 1:
                    sc.wait_ge(s_dve, B.at(("wq", wi - 1)))
                sc.activation(out=auq[:], in_=wload[:], func=AF.Abs)
                sc.drain()
                sc.activation(out=sgq[:], in_=wload[:], func=AF.Sign)
                sc.drain().then_inc(s_act, 1)
                B.tag(s_act, ("wabs", wi))
                sc.wait_ge(s_pe, B.at(("wT", wi)))
                for j in range(8):
                    bank = pT0 if j < 4 else pT1
                    slot = (j % 4) * 128
                    sc.copy(out=w6T[:, j * D + wi * 128:j * D + wi * 128 + 128],
                            in_=bank[:, slot:slot + 128])
                sc.drain().then_inc(s_act, 1)
                B.tag(s_act, ("wev", wi))
            for ci, (cs, cw) in enumerate(CH):
                for s in range(4):
                    sc.wait_ge(s_x[ci % 2], 16 * (ci // 2 + 1))
                    if (ci, s) != (0, 0):
                        prev = (ci, s - 1) if s > 0 else (ci - 1, 3)
                        sc.wait_ge(s_dve, B.at(("q6",) + prev))
                    else:
                        sc.wait_ge(s_dve, B.at(("wq", 7)))
                    xa = x3c[:, (ci % 2) * 4 * D + s * D:(ci % 2) * 4 * D + (s + 1) * D]
                    sc.activation(out=auq[:], in_=xa, func=AF.Abs)
                    sc.drain()
                    sc.activation(out=sgq[:], in_=xa, func=AF.Sign)
                    sc.drain().then_inc(s_act, 1)
                    B.tag(s_act, ("xas", ci, s))
                    sc.wait_ge(s_pe, B.at(("qT", ci, s)))
                    if (ci, s) != (0, 0):
                        prev = (ci, s - 1) if s > 0 else (ci - 1, 3)
                        sc.wait_ge(s_pe, B.at(("out1",) + prev))
                    for j in range(8):
                        bank = pT0 if j < 4 else pT1
                        slot = (j % 4) * 128
                        sc.copy(out=q6b[:, j * 128:(j + 1) * 128], in_=bank[:, slot:slot + 128])
                    sc.drain().then_inc(s_act, 1)
                    B.tag(s_act, ("qev", ci, s))
                    sc.wait_ge(s_dve, B.at(("zpre", ci, s)))
                    sc.activation(out=dump6[:], in_=zpre4[:], func=AF.Square,
                                  accum_out=stat6[:, 2:3])
                    sc.drain().then_inc(s_act, 1)
                    B.tag(s_act, ("sq6", ci, s))
                    sc.wait_ge(s_dve, B.at(("var", ci, s)))
                    sc.activation(out=stat6[:, 4:5], in_=stat6[:, 3:4], func=AF.Ln)
                    sc.drain()
                    sc.activation(out=stat6[:, 5:6], in_=stat6[:, 4:5], func=AF.Exp, scale=0.5)
                    sc.drain().then_inc(s_act, 1)
                    B.tag(s_act, ("rstd6", ci, s))

        def f_tensor(te, B):
            te.wait_ge(s_dve, B.at("ident"))
            te.wait_ge(s_c, 16)
            te.matmul(pB[:, 0:512], ones1[:], biasr[0:1, 0:512], start=True, stop=True)
            last = te.matmul(pB[:, 512:1024], ones1[:], biasr[0:1, 512:1024],
                             start=True, stop=True)
            last.then_inc(s_pe, 1)
            B.tag(s_pe, "biasbc")
            for wi in jobs:
                te.wait_ge(s_dve, B.at(("wq", wi)))
                if wi >= 1:
                    te.wait_ge(s_act, B.at(("wev", wi - 1)))
                last = None
                for j in range(8):
                    bank = pT0 if j < 4 else pT1
                    slot = (j % 4) * 128
                    last = te.transpose(bank[:, slot:slot + 128],
                                        qw[:, j * 128:(j + 1) * 128], ident6[:])
                last.then_inc(s_pe, 1)
                B.tag(s_pe, ("wT", wi))
            for ci, (cs, cw) in enumerate(CH):
                for s in range(4):
                    te.wait_ge(s_dve, B.at(("q6", ci, s)))
                    if (ci, s) == (0, 0):
                        te.wait_ge(s_act, B.at(("wev", 7)))
                    if (ci, s) != (0, 0):
                        prev = (ci, s - 1) if s > 0 else (ci - 1, 3)
                        te.wait_ge(s_act, B.at(("qev",) + prev))
                    last = None
                    for j in range(8):
                        bank = pT0 if j < 4 else pT1
                        slot = (j % 4) * 128
                        last = te.transpose(bank[:, slot:slot + 128],
                                            q6a[:, s * D + j * 128:s * D + (j + 1) * 128],
                                            ident6[:])
                    last.then_inc(s_pe, 1)
                    B.tag(s_pe, ("qT", ci, s))
                    te.wait_ge(s_act, B.at(("qev", ci, s)))
                    if (ci, s) != (0, 0):
                        prev = (ci, s - 1) if s > 0 else (ci - 1, 3)
                        te.wait_ge(s_dve, B.at(("zpre",) + prev))
                    for j in range(8):
                        te.matmul(pO0[:], q6b[:, j * 128:(j + 1) * 128],
                                  w6T[:, j * D:j * D + 512], start=(j == 0), stop=(j == 7))
                    last = None
                    for j in range(8):
                        last = te.matmul(pO1[:], q6b[:, j * 128:(j + 1) * 128],
                                         w6T[:, j * D + 512:j * D + 1024],
                                         start=(j == 0), stop=(j == 7))
                    last.then_inc(s_pe, 1)
                    B.tag(s_pe, ("out1", ci, s))

        two_pass(block, None, [("sync", f_sync), ("gpsimd", f_gpsimd),
                               ("vector", f_vector), ("scalar", f_scalar),
                               ("tensor", f_tensor)])
    return nc



_PROG_CACHE = {}


def _prog(name, fn, *args):
    key = (name,) + args
    if key not in _PROG_CACHE:
        _PROG_CACHE[key] = fn(*args)
    return _PROG_CACHE[key]


def kernel(**inp):
    import ml_dtypes
    x = inp["L0_attn_x"]
    Bb, T, D_ = x.shape
    assert D_ == D
    T_loc = T // 4
    n_cores = 8
    cores = list(range(n_cores))

    nc = _prog("A", build_stageA, T_loc, EX)
    im = [stageA_inputs(inp, T, cid) for cid in cores]
    res = run_bass_kernel_spmd(nc, im, core_ids=cores)
    z1 = np.zeros((Bb, T, D), np.float32)
    qxb = np.zeros((Bb, D, T), ml_dtypes.bfloat16)
    for cid in cores:
        b, t0, _ = shard_tokens(T, cid)
        z1[b, t0:t0 + T_loc] = res.results[cid]["z1_out"][EX:]
        qxb[b, :, t0:t0 + T_loc] = np.asarray(res.results[cid]["qxb_out"])[:, EX:]

    nc = _prog("B", build_stageB, T_loc, EX)
    im = [stageB_inputs(inp, z1, qxb, T, cid) for cid in cores]
    res = run_bass_kernel_spmd(nc, im, core_ids=cores)
    x2 = np.zeros((Bb, T, D), np.float32)
    for cid in cores:
        b, t0, _ = shard_tokens(T, cid)
        x2[b, t0:t0 + T_loc] = res.results[cid]["x2_out"][EX:]
    del z1, qxb

    nc = _prog("C1", build_stageC1, T_loc, EX)
    im = [stageC1_inputs(inp, x2, T, cid) for cid in cores]
    res = run_bass_kernel_spmd(nc, im, core_ids=cores)
    kg = np.zeros((Bb, D, T), np.float32)
    vg = np.zeros((Bb, D, T), np.float32)
    sr_cores = []
    for cid in cores:
        b, t0, _ = shard_tokens(T, cid)
        kg[b, :, t0:t0 + T_loc] = res.results[cid]["k_out"][:, EX:]
        vg[b, :, t0:t0 + T_loc] = res.results[cid]["v_out"][:, EX:]
        sr_cores.append(np.asarray(res.results[cid]["sr_out"]))

    nc = _prog("C2", build_stageC2, T)
    im = [stageC2_inputs(kg, vg, inp, cid) for cid in cores]
    res = run_bass_kernel_spmd(nc, im, core_ids=cores)
    yg = np.zeros((Bb, D, T), ml_dtypes.bfloat16)
    for cid in cores:
        yg[:, cid * 128:(cid + 1) * 128, :] = np.asarray(res.results[cid]["y_out"])
    del kg, vg

    nc = _prog("C3a", build_stageC3a, T_loc, EX)
    im = [stageC3a_inputs(inp, x2, yg, sr_cores[cid], T, cid) for cid in cores]
    res = run_bass_kernel_spmd(nc, im, core_ids=cores)
    s2m_c = [np.asarray(res.results[cid]["s2m_out"])[:, EX:] for cid in cores]
    gk_c = [np.asarray(res.results[cid]["gk_out"])[:, EX:] for cid in cores]
    sig_c = [np.asarray(res.results[cid]["sig_out"])[:, EX:] for cid in cores]
    del x2, yg, sr_cores

    nc = _prog("C3b", build_stageC3b, T_loc)
    im = [stageC3b_inputs(inp, np.ascontiguousarray(gk_c[cid]),
                          np.ascontiguousarray(sig_c[cid]),
                          np.ascontiguousarray(s2m_c[cid]), T, cid) for cid in cores]
    res = run_bass_kernel_spmd(nc, im, core_ids=cores)
    x3_c = [np.asarray(res.results[cid]["x3a_out"]) for cid in cores]
    del s2m_c, gk_c, sig_c

    nc = _prog("D", build_stageD, T_loc)
    bias_row = np.ascontiguousarray(inp["L1_ffn_bias"].reshape(1, D))
    im = [{"x3_in": x3_c[cid], "weight": inp["L1_ffn_weight"], "bias_row": bias_row}
          for cid in cores]
    res = run_bass_kernel_spmd(nc, im, core_ids=cores)
    out = np.zeros((Bb, T, D), np.float32)
    for cid in cores:
        b, t0, _ = shard_tokens(T, cid)
        out[b, t0:t0 + T_loc] = res.results[cid]["x4_out"]
    return out
